# version 67
# baseline (speedup 1.0000x reference)
import contextlib
import numpy as np
import ml_dtypes

import concourse.bass as bass
import concourse.mybir as mybir
from concourse.bass_utils import run_bass_kernel_spmd

F32 = mybir.dt.float32
BF16 = mybir.dt.bfloat16
AF = mybir.ActivationFunctionType
ALU = mybir.AluOpType
AX = mybir.AxisListType.X

NCORES = 8
B, S, D = 2, 8192, 1024
TPC = B * S // NCORES
HALO = 512
NCH = D // 128
TT = 512
ROUNDS = 2
FF = 4096
NFCH = FF // 128
EPS = 1e-5
MAGIC = float(np.float32(1.5 * 2.0 ** 23))
INV_D = 1.0 / float(D)

C_BIN, C_BOUT, C_GAM, C_BET, C_BPJ, C_LOGA, C_BP, C_CP = 0, 8, 16, 24, 32, 40, 48, 56


class Prog:

    ENGS = ("t", "v", "s", "g", "q")
    DSTREAMS = ("xd", "qd", "wfc", "wf2", "wpj", "wp2", "wd", "gd", "sd", "out", "l0", "l1", "l2", "b0", "b1")

    def __init__(self, nc, es):
        self.nc = nc
        self.sem = {e: es.enter_context(nc.semaphore(f"sem_{e}"))
                    for e in self.ENGS + self.DSTREAMS}
        self.cnt = {k: 0 for k in self.sem}
        self.ops = {e: [] for e in self.ENGS}
        self.writer = {}
        self.readers = {}
        self.dirty = {e: set() for e in self.ENGS}
        self.last_wait = {}
        self.max_waited = {}
        self.issued_floor = {}
        self.extra = {}

    def _edge(self, e, tok, waits, need_drain):
        sem_name, val, is_dma = tok
        if is_dma and sem_name == "qd":
            val = self.cnt["qd"]
        if sem_name == e and not is_dma:
            if val > 0:
                need_drain[0] = True
            return
        key = (e, sem_name)
        if self.last_wait.get(key, -1) >= val:
            return
        waits[sem_name] = max(waits.get(sem_name, 0), val)

    def emit(self, e, method, *args, reads=(), writes=(), sem=None, delta=None,
             accum=False, **kwargs):
        waits = {}
        need_drain = [False]
        for k in reads:
            w = self.writer.get(k)
            if w is not None:
                self._edge(e, w, waits, need_drain)
        for k in writes:
            if accum:
                continue
            for ok in self.extra.pop(k, ()):
                ow = self.writer.get(ok)
                if ow is not None:
                    self._edge(e, ow, waits, need_drain)
                for r in self.readers.get(ok, ()):
                    self._edge(e, r, waits, need_drain)
            w = self.writer.get(k)
            if w is not None:
                self._edge(e, w, waits, need_drain)
            for r in self.readers.get(k, ()):
                self._edge(e, r, waits, need_drain)
        if need_drain[0]:
            dirty_hit = any(k in self.dirty[e] for k in reads) or any(
                k in self.dirty[e] for k in writes)
            for k in writes:
                for r in self.readers.get(k, ()):
                    if r[0] == e and not r[2]:
                        dirty_hit = True
            need_drain[0] = dirty_hit

        is_dma = method == "dma_start"
        if need_drain[0]:
            self.dirty[e] = set()
        d = delta if delta is not None else (16 if is_dma else 1)
        sem_name = sem if sem is not None else e
        self.cnt[sem_name] += d
        tok = (sem_name, self.cnt[sem_name], is_dma)
        for k in writes:
            self.writer[k] = tok
            self.readers[k] = []
            self.dirty[e].discard(k)
        for k in reads:
            self.readers.setdefault(k, []).append(tok)
        if not is_dma:
            for k in writes:
                self.dirty[e].add(k)

        sem_h = self.sem[sem_name]
        wait_list = [(self.sem[sn], v) for sn, v in waits.items()]
        for sn, v in waits.items():
            self.last_wait[(e, sn)] = max(self.last_wait.get((e, sn), -1), v)
            self.max_waited[sn] = max(self.max_waited.get(sn, 0), v)
        if is_dma:
            if sem_name in ("xd", "wd", "gd", "out"):
                floor = self.cnt[sem_name] - d
            else:
                floor = self.max_waited.get(sem_name, 0)
            if floor > self.issued_floor.get(sem_name, 0):
                self.issued_floor[sem_name] = floor
                wait_list.insert(0, (self.sem[sem_name], floor))
        drain = need_drain[0]

        def thunk(eng, wait_list=wait_list, drain=drain, method=method,
                  args=args, kwargs=kwargs, sem_h=sem_h, d=d):
            for s_h, v in wait_list:
                eng.wait_ge(s_h, v)
            if drain:
                eng.drain()
            ins = getattr(eng, method)(*args, **kwargs)
            ins.then_inc(sem_h, d)

        self.ops[e].append(thunk)
        return tok

    def barrier(self):
        allsems = self.ENGS + self.DSTREAMS
        snap = {o: self.cnt[o] for o in allsems}
        for e in self.ENGS:
            def thunk(eng, e=e, snap=snap):
                for o in snap:
                    if o != e and snap[o] > 0:
                        eng.wait_ge(self.sem[o], snap[o])
                eng.drain()
            self.ops[e].append(thunk)
            self.dirty[e] = set()
            for o in allsems:
                if o != e:
                    self.last_wait[(e, o)] = max(
                        self.last_wait.get((e, o), -1), snap[o])
                    self.max_waited[o] = max(self.max_waited.get(o, 0), snap[o])

    def bump_stream(self, sem_name, keys):
        end = self.cnt[sem_name]
        for k in keys:
            if k in self.writer:
                self.writer[k] = (sem_name, end, True)

    def raw(self, e, fn):
        self.ops[e].append(fn)


def build_nc():
    nc = bass.Bass("TRN2", target_bir_lowering=False, debug=False)

    dt = lambda n, shp, dty, kind: nc.dram_tensor(n, shp, dty, kind=kind).ap()
    xT = dt("xT", [D, TPC], F32, "ExternalInput")
    xhT = dt("xhT", [D, HALO], F32, "ExternalInput")
    hmask = dt("hmask", [128, 1], F32, "ExternalInput")
    winT = dt("winT", [D, D], F32, "ExternalInput")
    woutT = dt("woutT", [D, D], F32, "ExternalInput")
    wfcT = dt("wfcT", [D, FF], F32, "ExternalInput")
    wprojT = dt("wprojT", [FF, D], F32, "ExternalInput")
    afc = dt("afc", [16, D], F32, "ExternalInput")
    bfcT = dt("bfcT", [16, FF], F32, "ExternalInput")
    aproj = dt("aproj", [16, FF], F32, "ExternalInput")
    bprojT = dt("bprojT", [16, D], F32, "ExternalInput")
    cvec = dt("cvec", [128, 64], F32, "ExternalInput")
    bfc2 = dt("bfc2", [128, 32], F32, "ExternalInput")
    ident = dt("ident", [128, 128], F32, "ExternalInput")
    ones1f = dt("ones1f", [1, 128], F32, "ExternalInput")
    onescb = dt("onescb", [128, 1], BF16, "ExternalInput")
    youtT = dt("youtT", [D, TPC], F32, "ExternalOutput")
    wfce = dt("wfce", [D, FF], BF16, "Internal")
    wproje = dt("wproje", [FF, D], BF16, "Internal")
    wine = dt("wine", [D, D], BF16, "Internal")
    woute = dt("woute", [D, D], BF16, "Internal")

    es = contextlib.ExitStack()
    with es:
        P = Prog(nc, es)
        sb = lambda n, shp, dty: es.enter_context(nc.sbuf_tensor(n, shp, dty))
        ps = lambda n, shp: es.enter_context(nc.psum_tensor(n, shp, F32))

        closed_regions = []

        def _span(h):
            m = nc.lookup_mloc(h)
            import math
            nb = 1
            for d in list(h.shape)[1:]:
                nb *= int(d)
            nb *= mybir.dt.size(h.dtype)
            return int(m.addr), int(m.addr) + nb

        def aopen(h, keys):
            a, b = _span(h)
            olds = set()
            for (s, e2, ks) in closed_regions:
                if s < b and a < e2:
                    olds |= ks
            olds -= set(keys)
            if olds:
                for k in keys:
                    P.extra.setdefault(k, set()).update(olds)
            return (a, b, frozenset(keys))

        def aclose(tokens):
            closed_regions.extend(tokens)

        sv = sb("sv", [128, 64], F32)
        sbfc = sb("sbfc", [128, 32], F32)
        sid = sb("sid", [128, 128], F32)
        so1f = sb("so1f", [33, 128], F32)
        socb = sb("socb", [128, 1], BF16)
        socd = sb("socd", [128, 1], BF16)
        shm = sb("shm", [128, 1], F32)
        sdecay = sb("sdecay", [128, 8], F32)
        sBb = sb("sBb", [128, 8], F32)
        sscal = sb("sscal", [128, 8], F32)
        samax = sb("samax", [128, 33], F32)
        srow_ab = sb("srow_ab", [33, 512], F32)
        smcol = sb("smcol", [128, 8], F32)
        sbpb = sb("sbpb", [128, 8], F32)

        pA = ps("pA", [128, 512])
        pB = ps("pB", [128, 512])
        pW = [ps(f"pW{i}", [128, 512]) for i in range(4)]
        pst0 = ps("pst0", [128, 512])
        pbc = ps("pbc", [128, 512])

        col = lambda base, c: sv[:, base + c:base + c + 1]

        small_keys = []
        for dst, src, key in [
            (sv[:], cvec, "sv"), (sbfc[:], bfc2, "sbfc"), (sid[:], ident, "sid"),
            (so1f[0:1, :], ones1f, "so1f"), (so1f[32:33, :], ones1f, "so1f"),
            (socb[:], onescb, "socb"), (shm[:], hmask, "shm"),
        ]:
            P.emit("q", "dma_start", out=dst, in_=src, writes=[key], sem="qd")
            small_keys.append(key)

        P.emit("s", "activation", socd[:], socb[:], AF.Identity,
               scale=np.float32(1.0 / D), reads=["socb"], writes=["socd"])

        P.emit("s", "activation", sdecay[:], sv[:, C_LOGA:C_LOGA + 8], AF.Exp,
               reads=["sv"], writes=["sdecay"])
        P.emit("s", "activation", sdecay[:], sdecay[:], AF.Ln, bias=np.float32(1.0),
               reads=["sdecay"], writes=["sdecay"])
        P.emit("s", "activation", sdecay[:], sdecay[:], AF.Exp, scale=np.float32(-1.0),
               reads=["sdecay"], writes=["sdecay"])
        P.emit("v", "tensor_tensor", sBb[:], sv[:, C_BIN:C_BIN + 8],
               sv[:, C_BP:C_BP + 8], ALU.mult, reads=["sv"], writes=["sBb"])

        def prep_weight(wsrc, wdst, nchunks, fwidth, a_sb, a_key, bT_sb, bT_key,
                        col_recip, col_s, tag, wraw, qbuf0, qbuf1,
                        abt_full, abt_full2, chunk_hook=None, preloaded=False):
            if True:
                wbig = wraw[:].rearrange("p (c f) -> p c f", c=nchunks)
                wbig_keys = ([f"wbig_{tag}_g{g}" for g in range(nchunks // 4)]
                             + [f"wbig_{tag}{c}" for c in range(nchunks)])
                wsrc_r = wsrc.rearrange("(c p) f -> p c f", p=128)
                GRP = 2 if nchunks == NCH else 4
                ngrp = nchunks // GRP
                lstream = lambda g: ("l0", "l1", "l2")[g % 3]
                if not preloaded:
                    for g0 in range(min(3, ngrp)):
                        sl = slice(g0 * GRP, (g0 + 1) * GRP)
                        P.emit("q", "dma_start", out=wbig[:, sl, :],
                               in_=wsrc_r[:, sl, :],
                               writes=[f"wbig_{tag}_g{g0}"], sem=lstream(g0))
                    for g0 in range(ngrp):
                        for c in range(g0 * GRP, (g0 + 1) * GRP):
                            P.emit("v", "tensor_reduce", samax[:, c:c + 1],
                                   wbig[:, c, :], AX, ALU.max,
                                   apply_absolute_value=True,
                                   reads=[f"wbig_{tag}_g{g0}"],
                                   writes=["samax"])
                        if g0 + 3 < ngrp:
                            sl = slice((g0 + 3) * GRP, (g0 + 4) * GRP)
                            P.emit("q", "dma_start", out=wbig[:, sl, :],
                                   in_=wsrc_r[:, sl, :],
                                   writes=[f"wbig_{tag}_g{g0 + 3}"],
                                   sem=lstream(g0 + 3))
                P.emit("v", "tensor_reduce", samax[:, 32:33], samax[:, 0:nchunks],
                       AX, ALU.max, apply_absolute_value=True,
                       reads=["samax"], writes=["samax1"])
                P.emit("t", "transpose", pst0[0:1, 0:128], samax[:, 32:33], sid[:],
                       reads=["samax1", "sid"], writes=["pst0"])
                C_S, C_R = srow_ab[0:1, 0:1], srow_ab[0:1, 1:2]
                P.emit("v", "tensor_reduce", C_S, pst0[0:1, 0:128],
                       AX, ALU.max, reads=["pst0"], writes=["srow"])
                P.emit("v", "reciprocal", C_R, C_S,
                       reads=["srow"], writes=["srow"])
                P.emit("v", "tensor_single_scalar", C_R, C_R,
                       127.0, ALU.mult, reads=["srow"], writes=["srow"])
                P.emit("v", "tensor_single_scalar", C_S, C_S,
                       1.0 / 127.0, ALU.mult, reads=["srow"], writes=["srow"])
                P.emit("t", "matmul", pbc[:, 0:1], so1f[0:1, :],
                       C_R,
                       start=True, stop=True, reads=["so1f", "srow"], writes=["pbc"])
                P.emit("v", "tensor_copy", sscal[:, col_recip:col_recip + 1],
                       pbc[:, 0:1], reads=["pbc"], writes=["sscal"])
                P.emit("t", "matmul", pbc[:, 0:1], so1f[0:1, :],
                       C_S,
                       start=True, stop=True, reads=["so1f", "srow"], writes=["pbc"])
                P.emit("v", "tensor_copy", sscal[:, col_s:col_s + 1],
                       pbc[:, 0:1], reads=["pbc"], writes=["sscal"])
                if tag == "pj":
                    for h in range(4):
                        sl = slice(h * D, (h + 1) * D)
                        P.emit("q", "dma_start", out=a_sb[:],
                               in_=aproj[:, sl], reads=[a_key],
                               writes=[a_key], sem="qd")
                        P.emit("v", "tensor_scalar", abf[a_key][:, sl],
                               a_sb[:],
                               sscal[0:16, col_recip:col_recip + 1], 2.0,
                               ALU.mult, op1=ALU.mult,
                               reads=[a_key, "sscal"], writes=[a_key + "b"])
                else:
                    P.emit("v", "tensor_scalar", abf[a_key][:], a_sb[:],
                           sscal[0:16, col_recip:col_recip + 1], 2.0, ALU.mult,
                           op1=ALU.mult, reads=[a_key, "sscal"],
                           writes=[a_key + "b"])

                nseg = fwidth // 512
                for c in range(nchunks):
                    qbuf = (qbuf0 if c % 2 == 0 else qbuf1)[:, 0:fwidth]
                    qk = f"qbuf{c % 2}"
                    abt = (abt_full if c % 2 == 0 else abt_full2)[:, 0:fwidth]
                    abk = "abt" if c % 2 == 0 else "abt2"
                    P.emit("s", "mul", wbig[:, c, :], wbig[:, c, :],
                           sscal[:, col_recip:col_recip + 1],
                           reads=[f"wbig_{tag}_g{c // GRP}", "sscal"],
                           writes=[f"wbig_{tag}{c}"])
                    P.emit("v", "tensor_scalar", qbuf, wbig[:, c, :],
                           MAGIC, MAGIC, ALU.add, op1=ALU.subtract,
                           reads=[f"wbig_{tag}{c}"], writes=[qk])
                    for n in range(nseg):
                        ring = [(pA, "pA"), (pB, "pB"), (pW[0], "pW0"),
                                (pW[1], "pW1"), (pW[2], "pW2"), (pW[3], "pW3")]
                        pX, pk = ring[(c * nseg + n) % 6]
                        sl = slice(n * 512, (n + 1) * 512)
                        P.emit("t", "matmul", pX[:],
                               abf[a_key][:, c * 128:(c + 1) * 128],
                               abf[bT_key][:, sl], start=True, stop=True,
                               reads=[a_key + "b", bT_key + "b"], writes=[pk])
                        P.emit("s", "copy", abt[:, sl], pX[:],
                               reads=[pk], writes=[abk])
                    P.emit("v", "tensor_tensor", qbuf, abt,
                           qbuf, ALU.add,
                           reads=[abk, qk], writes=[qk])
                    P.emit("g", "dma_start",
                           out=wdst[c * 128:(c + 1) * 128, :], in_=qbuf,
                           reads=[qk], writes=[f"wdst_{tag}_{c}"],
                           sem=("wfc" if c % 2 == 0 else "wf2") if tag == "fc"
                               else ("wpj" if c % 2 == 0 else "wp2"))
                    if chunk_hook is not None:
                        chunk_hook(c)

        with nc.sbuf_tensor("safc", [16, D], F32) as safc, \
             nc.sbuf_tensor("safcb", [16, D], BF16) as safcb, \
             nc.sbuf_tensor("sbfcTb", [16, FF], BF16) as sbfcTb, \
             nc.sbuf_tensor("saprojb", [16, FF], BF16) as saprojb, \
             nc.sbuf_tensor("sbprojTb", [16, D], BF16) as sbprojTb, \
             nc.sbuf_tensor("wraw", [128, 32768], F32) as wraw, \
             nc.sbuf_tensor("qbuf0", [128, FF], BF16) as qbuf0, \
             nc.sbuf_tensor("qbuf1", [128, FF], BF16) as qbuf1, \
             nc.sbuf_tensor("abt", [128, FF], BF16) as abt_full, \
             nc.sbuf_tensor("abt2", [128, FF], BF16) as abt_full2:
            fc_keys = ([f"wbig_fc_g{g}" for g in range(4)]
                       + [f"wbig_fc{c}" for c in range(NCH)])
            pj_keys = ([f"wbig_pj_g{g}" for g in range(8)]
                       + [f"wbig_pj{c}" for c in range(NFCH)])
            _abk = [aopen(safc, ["safc"]),
                    aopen(safcb, ["safcb"]), aopen(sbfcTb, ["sbfcTb"]),
                    aopen(saprojb, ["saprojb"]), aopen(sbprojTb, ["sbprojTb"]),
                    aopen(wraw, fc_keys + pj_keys),
                    aopen(qbuf0, ["qbuf0"]), aopen(qbuf1, ["qbuf1"]),
                    aopen(abt_full, ["abt"]), aopen(abt_full2, ["abt2"])]
            P.emit("q", "dma_start", out=safc[:], in_=afc, writes=["safc"],
                   sem="qd")
            small_keys.append("safc")
            P.bump_stream("qd", small_keys)
            abf = {"safc": safcb, "sbfcT": sbfcTb,
                   "saproj": saprojb, "sbprojT": sbprojTb}
            P.emit("g", "dma_start", out=sbfcTb[:], in_=bfcT,
                   writes=["sbfcTb"], sem="b0")
            P.emit("g", "dma_start", out=sbprojTb[:], in_=bprojT,
                   writes=["sbprojTb"], sem="b1")
            with nc.sbuf_tensor("wstg", [128, 1, D], F32) as wstg, \
                 nc.sbuf_tensor("wstgb", [128, 4, D], BF16) as wstgb:
                _tk = [aopen(wstg, ["wstg"]), aopen(wstgb, ["wstgb"])]
                for wsrc, wdst, tg in [(winT, wine, "wi"),
                                       (woutT, woute, "wo")]:
                    wr = wsrc.rearrange("(c p) f -> p c f", p=128)
                    wd = wdst.rearrange("(c p) f -> p c f", p=128)
                    for h in range(2):
                        for j in range(4):
                            c = 4 * h + j
                            P.emit("g", "dma_start", out=wstg[:],
                                   in_=wr[:, c:c + 1, :],
                                   reads=["wstg"], writes=["wstg"], sem="gd")
                            P.emit("s", "activation", wstgb[:, j, :],
                                   wstg[:, 0, :], AF.Identity,
                                   reads=["wstg"], writes=["wstgb"])
                        P.emit("g", "dma_start",
                               out=wd[:, 4 * h:4 * h + 4, :],
                               in_=wstgb[:], reads=["wstgb"],
                               writes=[f"wsc_{tg}"], sem="gd")
                aclose(_tk)
            wpj_src = wprojT.rearrange("(c p) f -> p c f", p=128)
            wbig_pj = wraw[:].rearrange("p (c f) -> p c f", c=NFCH)

            def pj_reduces(g):
                for c in range(g * 4, (g + 1) * 4):
                    P.emit("v", "tensor_reduce", samax[:, c:c + 1],
                           wbig_pj[:, c, :], AX, ALU.max,
                           apply_absolute_value=True,
                           reads=[f"wbig_pj_g{g}"], writes=["samax"])

            def chase(cfc):
                g = cfc
                P.extra.setdefault(f"wbig_pj_g{g}", set()).update(
                    {f"wbig_fc{g}", f"wbig_fc_g{g // 2}"})
                sl = slice(g * 4, (g + 1) * 4)
                P.emit(("q", "q", "g")[g % 3], "dma_start",
                       out=wbig_pj[:, sl, :], in_=wpj_src[:, sl, :],
                       writes=[f"wbig_pj_g{g}"],
                       sem=("l0", "l1", "b0")[g % 3])
                if g >= 2:
                    pj_reduces(g - 2)

            prep_weight(wfcT, wfce, NCH, FF, safc, "safc", None, "sbfcT",
                        0, 1, "fc", wraw, qbuf0, qbuf1, abt_full, abt_full2,
                        chunk_hook=chase)
            pj_reduces(6)
            pj_reduces(7)
            with nc.sbuf_tensor("aptmp", [16, D], F32) as aptmp:
                _apk = [aopen(aptmp, ["saproj"])]
                prep_weight(wprojT, wproje, NFCH, D, aptmp, "saproj",
                            None, "sbprojT", 2, 3, "pj", wraw, qbuf0, qbuf1,
                            abt_full, abt_full2, preloaded=True)
                aclose(_apk)
            aclose(_abk)

        smub = sb("smub", [128, TT], F32)
        sab = sb("sab", [128, TT], F32)
        sz = sb("sz", [128, NCH, 2 * TT], F32)

        def bc3(ap2d):
            return ap2d.rearrange("p (a n) -> p a n", a=1).broadcast_to(
                [128, NCH, TT])

        def colb3(base):
            return sv[:, base:base + 8].rearrange(
                "p (c a) -> p c a", a=1).broadcast_to([128, NCH, TT])

        def decb(c):
            return sdecay[:, c:c + 1].broadcast_to([128, TT])

        def ln_stats(src_ap, src_keys, ssq, sq_key, sob, ob_key, sq_eng="s",
                     mid_hook=None):
            sq_keys = sq_key if isinstance(sq_key, list) else [sq_key]
            ob_keys = ob_key if isinstance(ob_key, list) else [ob_key]
            """Feature-LN stats of src -> broadcast tiles sab (a) and smub (mu*a).

            src_ap: [128, NCH, TT] view (possibly strided). Emits casts on s/g,
            stats matmuls on t, row math spread over s/g/v, broadcasts on t,
            evacuations on s.
            """
            if sq_eng == "s":
                P.emit("s", "activation", ssq[:], src_ap, AF.Square,
                       reads=src_keys, writes=sq_keys)
            else:
                P.emit("v", "tensor_tensor", ssq[:], src_ap, src_ap,
                       ALU.mult, reads=src_keys, writes=sq_keys)
            P.emit("v", "tensor_copy", sob[:], src_ap,
                   reads=src_keys, writes=ob_keys)
            for c in range(NCH):
                P.emit("t", "matmul", pst0[0:1, :], socd[:], sob[:, c, :],
                       start=(c == 0), stop=(c == NCH - 1), accum=(c > 0),
                       tile_position=(0, 0),
                       reads=["socd"] + ob_keys, writes=["pst0"])
                P.emit("t", "matmul", pst0[32:33, :], socd[:], ssq[:, c, :],
                       start=(c == 0), stop=(c == NCH - 1), accum=(c > 0),
                       tile_position=(0, 32),
                       reads=["socd"] + sq_keys, writes=["pst0"])
            MU, MSQ, VAR = pst0[0:1, :], pst0[32:33, :], pst0[96:97, :]
            A_, MUA = srow_ab[0:1, :], srow_ab[32:33, :]
            P.emit("s", "activation", A_, MU, AF.Square,
                   reads=["pst0"], writes=["srow"])
            P.emit("v", "tensor_tensor", VAR, MSQ, A_,
                   ALU.subtract, reads=["pst0", "srow"], writes=["pst0"])
            P.emit("s", "activation", A_, VAR, AF.Ln,
                   bias=np.float32(EPS), reads=["pst0"], writes=["srow"])
            P.emit("s", "activation", A_, A_, AF.Exp,
                   scale=np.float32(-0.5), reads=["srow"], writes=["srow"])
            P.emit("v", "tensor_tensor", MUA, MU, A_,
                   ALU.mult, reads=["pst0", "srow"], writes=["srow"])
            if mid_hook is not None:
                mid_hook()
            P.emit("t", "matmul", pbc[:], so1f[0:1, :], A_,
                   start=True, stop=True, reads=["so1f", "srow"], writes=["pbc"])
            P.emit("s", "copy", sab[:], pbc[:], reads=["pbc"], writes=["sab"])
            P.emit("t", "matmul", pbc[:], so1f[32:33, :], MUA,
                   start=True, stop=True, reads=["so1f", "srow"], writes=["pbc"])
            P.emit("s", "copy", smub[:], pbc[:], reads=["pbc"], writes=["smub"])

        sxt1 = sb("sxt1", [128, NCH, TT], F32)
        sbfA1 = sb("sbfA1", [128, NCH, TT], BF16)
        sut1 = sb("sut1", [128, NCH, TT], BF16)
        swin = sb("swin", [128, NCH, D], BF16)
        sstb = sb("sstb", [128, NCH, TT], F32)
        sxt0 = sb("sxt0", [128, NCH, TT], F32)
        sbfA0 = sb("sbfA0", [128, NCH, TT], BF16)
        sut0 = sb("sut0", [128, NCH, TT], BF16)
        swout = sb("swout", [128, NCH, D], BF16)
        sot = sb("sot", [128, NCH, TT], F32)
        sscr = sb("sscr", [128, NCH, TT], BF16)
        sG = sb("sG", [128, 16, TT], BF16)
        s1 = sb("s1", [128, NCH, 1024], BF16)
        assert nc.sbuf_base <= 229344, nc.sbuf_base

        addr = lambda h: nc.lookup_mloc(h).addr
        assert addr(sut0) == addr(sbfA0) + 8192
        assert addr(sut1) == addr(sbfA1) + 8192
        gat = nc.alloc_sbuf_tensor_at
        gwin = gat("gwin", [128, NCH, 1024], BF16, offset=addr(swin))
        gwout = gat("gwout", [128, NCH, 1024], BF16, offset=addr(swout))
        gx0 = gat("gx0", [128, NCH, 1024], BF16, offset=addr(sxt0))
        gb0 = gat("gb0", [128, NCH, 1024], BF16, offset=addr(sbfA0))
        gx1 = gat("gx1", [128, NCH, 1024], BF16, offset=addr(sxt1))
        gb1 = gat("gb1", [128, NCH, 1024], BF16, offset=addr(sbfA1))
        gq1a = gat("gq1a", [128, 4, 1024], BF16, offset=addr(sut0))
        gq1b = gat("gq1b", [128, 4, 1024], BF16, offset=addr(sut1))
        gsq = gat("gsq", [128, NCH, TT], BF16, offset=addr(sstb))
        gob = gat("gob", [128, NCH, TT], BF16, offset=addr(sstb) + 8192)
        gaf = gat("gaf", [128, NCH, TT], F32, offset=addr(sot))

        sxt2, sbfA2, sut2 = [sxt0, sxt1], [sbfA0, sbfA1], [sut0, sut1]
        su0k = [f"su0_{e}" for e in range(NCH)]
        su1k = [f"su1_{e}" for e in range(NCH)]
        sstk = [f"sst{c}" for c in range(NCH)]
        afk = [f"af{j}" for j in range(NCH)]
        fck = [f"wdst_fc_{c}" for c in range(NCH)]
        pjk = [f"wdst_pj_{c}" for c in range(NFCH)]
        _pk = [aopen(swin, ["swin"]), aopen(swout, ["swout"]),
               aopen(sxt0, ["sxt0"]), aopen(sxt1, ["sxt1"]),
               aopen(sbfA0, ["bfA0"]), aopen(sbfA1, ["bfA1"]),
               aopen(sut0, su0k), aopen(sut1, su1k),
               aopen(sstb, sstk), aopen(sot, ["sot"]),
               aopen(sscr, ["sscr0", "sscr1"]), aopen(sG, ["sG"]),
               aopen(s1, ["s1"]), aopen(smub, ["smub"]),
               aopen(sab, ["sab"]),
               aopen(sz, [f"sz{i}_{c}" for i in range(2) for c in range(NCH)])]

        xr = xT.rearrange("(c p) n -> p c n", p=128)
        wfc_d = wfce.rearrange("(c p) f -> p c f", p=128)
        wpj_d = wproje.rearrange("(c p) f -> p c f", p=128)
        win_d = wine.rearrange("(c p) f -> p c f", p=128)
        wout_d = woute.rearrange("(c p) f -> p c f", p=128)
        yout_d = youtT.rearrange("(c p) n -> p c n", p=128)

        def load_x(n, src_ap):
            p = n % 2
            P.emit("q", "dma_start", out=sxt2[p][:], in_=src_ap,
                   reads=[f"sxt{p}", f"bfA{p}"], writes=[f"sxt{p}"], sem="xd")

        def stage_A_evac(p, e):
            pX, pk = (pA, "pA") if e % 2 == 0 else (pB, "pB")
            P.emit("s", "activation", sut2[p][:, e, :], pX[:],
                   AF.Identity, bias=sBb[:, e:e + 1], scale=col(C_BP, e),
                   reads=[pk, "sBb", "sv"], writes=[f"su{p}_{e}"])

        def stage_A(n, add_beta=True, es=None, emit_cast=True,
                    emit_evac=True):
            p = n % 2
            xk, bk = f"sxt{p}", f"bfA{p}"
            if emit_cast:
                P.emit("s", "activation", sbfA2[p][:], sxt2[p][:],
                       AF.Identity, reads=[xk], writes=[bk])
            if add_beta:
                for c in range(NCH):
                    P.emit("s", "activation", sxt2[p][:, c, :],
                           sxt2[p][:, c, :], AF.Identity,
                           bias=col(C_BET, c),
                           reads=[xk, "sv"], writes=[xk])
            for e in (range(NCH) if es is None else es):
                pX, pk = (pA, "pA") if e % 2 == 0 else (pB, "pB")
                for k in range(NCH):
                    P.emit("t", "matmul", pX[:],
                           swin[:, k, e * 128:(e + 1) * 128],
                           sbfA2[p][:, k, :],
                           start=(k == 0), stop=(k == NCH - 1),
                           accum=(k > 0), reads=["swin", bk], writes=[pk])
                if emit_evac:
                    stage_A_evac(p, e)

        def stage_B(n):
            p = n % 2
            for c in range(NCH):
                P.emit("v", "tensor_tensor_scan", sstb[:, c, :],
                       sdecay[:, c:c + 1].broadcast_to([128, TT]),
                       sut2[p][:, c, :], smcol[:, c:c + 1],
                       ALU.mult, ALU.add,
                       reads=["sdecay", f"su{p}_{c}", f"sst{c}", f"smcol{c}"],
                       writes=[f"sst{c}"])
            P.emit("v", "tensor_copy", smcol[:, 0:NCH],
                   sstb[:, :, TT - 1:TT].rearrange("p c a -> p (c a)"),
                   reads=sstk,
                   writes=[f"smcol{c}" for c in range(NCH)])

        def stage_y(n):
            p = n % 2
            P.emit("v", "tensor_tensor", sbfA2[p][:], sstb[:], colb3(C_CP),
                   ALU.mult, reads=sstk + ["sv"], writes=[f"bfA{p}"])

        def stage_C_evac(e):
            i = e % 4
            P.emit("s", "activation", sot[:, e, :], pW[i][:],
                   AF.Identity, bias=col(C_BOUT, e),
                   reads=[f"pW{i}", "sv"], writes=["sot"])

        def stage_C(n, es=None, emit_evac=True):
            p = n % 2
            bk = f"bfA{p}"
            for e in (range(NCH) if es is None else es):
                i = e % 4
                for k in range(NCH):
                    P.emit("t", "matmul", pW[i][:],
                           swout[:, k, e * 128:(e + 1) * 128],
                           sbfA2[p][:, k, :],
                           start=(k == 0), stop=(k == NCH - 1),
                           accum=(k > 0), reads=["swout", bk], writes=[f"pW{i}"])
                if emit_evac:
                    stage_C_evac(e)

        def stage_D_head(p, mid_hook=None):
            ln_stats(sot[:], ["sot"], gsq, sstk[0:4], gob[:], sstk[4:8],
                     mid_hook=mid_hook)
            for c in range(NCH):
                P.emit("v", "scalar_tensor_tensor", sot[:, c, :],
                       sot[:, c, :], col(C_GAM, c), sab[:],
                       ALU.mult, ALU.mult,
                       reads=["sot", "sv", "sab"], writes=["sot"])
            for c in range(NCH):
                P.emit("v", "scalar_tensor_tensor", sstb[:, c, :],
                       smub[:], col(C_GAM, c), sxt2[p][:, c, :],
                       ALU.mult, ALU.subtract,
                       reads=["smub", "sv", f"sxt{p}"] + sstk,
                       writes=[f"sst{c}"])
            P.emit("v", "tensor_tensor", sxt2[p][:], sot[:], sstb[:],
                   ALU.subtract, reads=["sot"] + sstk, writes=[f"sxt{p}"])

        def stage_D_tail(p, it, mid_hook=None):
            zsl = slice(it * TT, (it + 1) * TT)
            zkeys = [f"sz{it}_{c}" for c in range(NCH)]
            ln_stats(sxt2[p][:], [f"sxt{p}"], gsq, sstk[0:4], gob[:],
                     sstk[4:8], sq_eng="v", mid_hook=mid_hook)
            for h in range(2):
                cs = slice(4 * h, 4 * h + 4)
                hb = lambda ap2d: ap2d.rearrange(
                    "p (a n) -> p a n", a=1).broadcast_to([128, 4, TT])
                P.emit("v", "tensor_tensor", sz[:, cs, zsl], sxt2[p][:, cs, :],
                       hb(sab[:]), ALU.mult,
                       reads=[f"sxt{p}", "sab"], writes=zkeys[4 * h:4 * h + 4])
                P.emit("v", "tensor_tensor", sz[:, cs, zsl], sz[:, cs, zsl],
                       hb(smub[:]), ALU.subtract,
                       reads=zkeys[4 * h:4 * h + 4] + ["smub"],
                       writes=zkeys[4 * h:4 * h + 4])

        FCRING = [(pA, "pA"), (pB, "pB"), (pW[0], "pW0"), (pW[1], "pW1")]
        PJRING = [(pW[2], "pW2"), (pW[3], "pW3"), (pA, "pA"), (pB, "pB")]

        def fc_quarter(q, pieces, jjs=None):
            for jj in (range(8) if jjs is None else jjs):
                m = 8 * q + jj
                pX, pk = FCRING[m % 4]
                for k in range(NCH):
                    sl, ky, klo = next((s, y, lo) for (s, y, lo, hi) in pieces
                                       if lo <= k < hi)
                    P.emit("t", "matmul", pX[:],
                           sl[:, k - klo, jj * 128:(jj + 1) * 128],
                           sscr[:, k, :],
                           start=(k == 0), stop=(k == NCH - 1),
                           accum=(k > 0),
                           reads=[ky, f"sscr{k // 4}"], writes=[pk])
                P.emit("s", "activation", sG[:, m % 16, :], pX[:], AF.Gelu,
                       bias=sbfc[:, m:m + 1], scale=sscal[:, 1:2],
                       reads=[pk, "sbfc", "sscal"], writes=["sG"])

        def pj_half(sub, slotL, keyL, slotH, keyH, first, js=None,
                    emit_evac=True):
            zsl = slice(sub * TT, (sub + 1) * TT)
            for j in (range(NCH) if js is None else js):
                pX, pk = PJRING[j % 4]
                korder = (list(range(8, 16)) + list(range(8))) if first \
                    else list(range(16))
                for idx, k in enumerate(korder):
                    sl, ky = (slotL, keyL) if k < 8 else (slotH, keyH)
                    P.emit("t", "matmul", pX[:],
                           sl[:, k % 8, j * 128:(j + 1) * 128], sG[:, k, :],
                           start=(idx == 0), stop=(idx == 15),
                           accum=(idx > 0), reads=[ky, "sG"], writes=[pk])
                if emit_evac:
                    pj_evac(sub, j, first)

        def pj_evac(sub, j, first):
            zsl = slice(sub * TT, (sub + 1) * TT)
            pX, pk = PJRING[j % 4]
            if first:
                P.emit("s", "activation", gaf[:, j, :], pX[:], AF.Identity,
                       bias=sbpb[:, j:j + 1],
                       reads=[pk, "sbpb"], writes=[afk[j]])
            else:
                P.emit("v", "tensor_tensor", gaf[:, j, :], pX[:],
                       gaf[:, j, :], ALU.add,
                       reads=[pk, afk[j]], writes=[afk[j]])
                P.emit("v", "scalar_tensor_tensor", sz[:, j, zsl],
                       gaf[:, j, :], sscal[:, 3:4], sz[:, j, zsl],
                       ALU.mult, ALU.add,
                       reads=[afk[j], "sscal", f"sz{sub}_{j}"],
                       writes=[f"sz{sub}_{j}"])

        def szb_cast(sub):
            zsl = slice(sub * TT, (sub + 1) * TT)
            for h in range(2):
                cs = slice(4 * h, 4 * h + 4)
                P.emit("s", "activation", sscr[:, cs, :], sz[:, cs, zsl],
                       AF.Identity,
                       reads=[f"sz{sub}_{c}" for c in range(4 * h, 4 * h + 4)]
                       + [f"sscr{h}"],
                       writes=[f"sscr{h}"])

        def prebias(sub):
            zsl = slice(sub * TT, (sub + 1) * TT)
            for c in range(NCH):
                P.emit("s", "activation", sz[:, c, zsl], sz[:, c, zsl],
                       AF.Identity, bias=col(C_BPJ, c),
                       reads=[f"sz{sub}_{c}", "sv", "sscr0", "sscr1"],
                       writes=[f"sz{sub}_{c}"])

        def ln3_fire(r, sub, mid_hook=None):
            zsl = slice(sub * TT, (sub + 1) * TT)
            zkeys = [f"sz{sub}_{c}" for c in range(NCH)]
            toff = r * 2 * TT + sub * TT
            ln_stats(sz[:, :, zsl], zkeys, gsq, sstk[0:4], gob[:], sstk[4:8],
                     mid_hook=mid_hook)
            for h in range(2):
                cs = slice(4 * h, 4 * h + 4)
                hb = lambda ap2d: ap2d.rearrange(
                    "p (a n) -> p a n", a=1).broadcast_to([128, 4, TT])
                hk = afk[4 * h:4 * h + 4]
                P.emit("v", "tensor_tensor", gaf[:, cs, :], sz[:, cs, zsl],
                       hb(sab[:]), ALU.mult,
                       reads=zkeys[4 * h:4 * h + 4] + ["sab"], writes=hk)
                P.emit("v", "tensor_tensor", gaf[:, cs, :], gaf[:, cs, :],
                       hb(smub[:]), ALU.subtract,
                       reads=hk + ["smub"], writes=hk)
                P.emit("g", "dma_start", out=yout_d[:, cs, toff:toff + TT],
                       in_=gaf[:, cs, :], reads=hk,
                       writes=[f"yout{r}{sub}{h}"], sem="out")

        def wload(eng, sem, dst, src, wkeys, hkeys):
            P.emit(eng, "dma_start", out=dst[:], in_=src,
                   reads=wkeys + hkeys, writes=hkeys, sem=sem)

        def emit_mega(r):
            for j in range(NCH):
                P.extra.setdefault(afk[j], set()).add("sot")
            P.extra.setdefault("sot", set()).update(afk)
            for e in range(NCH):
                P.extra.setdefault(f"su0_{e}", set()).add("bfA0")
                P.extra.setdefault(f"su1_{e}", set()).add("bfA1")

            t0 = r * 2 * TT
            ldq = "s" if r == 0 else "q"
            ldg = "s" if r == 0 else "g"

            def load_x0(n, src_ap):
                p = n % 2
                P.emit(ldq, "dma_start", out=sxt2[p][:], in_=src_ap,
                       reads=[f"sxt{p}", f"bfA{p}"], writes=[f"sxt{p}"],
                       sem="xd")

            if r == 0:
                load_x0(1, xhT.rearrange("(c p) n -> p c n", p=128))
                load_x0(0, xr[:, :, t0:t0 + TT])
                wload(ldq, "wd", swin, win_d, ["wsc_wi"], ["swin"])
            if r > 0:
                wload("g", "gd", swout, wout_d, ["wsc_wo"], ["swout"])
            if r == 0:
                stage_A(-1, add_beta=False)
                for c in range(NCH):
                    P.emit("v", "tensor_tensor_scan", sstb[:, c, :],
                           sdecay[:, c:c + 1].broadcast_to([128, TT]),
                           sut1[:, c, :], 0.0, ALU.mult, ALU.add,
                           reads=["sdecay", f"su1_{c}", f"sst{c}"],
                           writes=[f"sst{c}"])
                for c in range(NCH):
                    P.emit("v", "tensor_tensor", smcol[:, c:c + 1],
                           sstb[:, c, HALO - 1:HALO], shm[:], ALU.mult,
                           reads=[f"sst{c}", "shm"], writes=[f"smcol{c}"])
                load_x0(1, xr[:, :, t0 + TT:t0 + 2 * TT])
            if r == 0:
                stage_A(0)
            else:
                for _e in (2, 3):
                    stage_A_evac(0, _e)
                stage_A(0, emit_cast=False, es=range(4, NCH))
            stage_A(1)
            if r == 0:
                wload(ldg, "wd", swout, wout_d, ["wsc_wo"], ["swout"])
            wload("q", "wd", s1, wfc_d[:, :, 0:1024], fck, ["s1"])
            wload("q", "wd", gwin, wfc_d[:, :, 2048:3072], fck, ["swin"])
            stage_B(0)
            wload("g", "gd", gq1a, wfc_d[:, 0:4, 1024:2048], fck, su0k)
            stage_y(0)
            stage_C(0)
            stage_B(1)
            wload("g", "gd", gq1b, wfc_d[:, 4:8, 1024:2048], fck, su1k)
            stage_y(1)
            stage_D_head(0, mid_hook=lambda: stage_C(
                1, es=[0, 1], emit_evac=False))
            stage_C_evac(0)
            stage_C_evac(1)
            stage_C(1, es=range(2, 6))
            stage_D_tail(0, 0, mid_hook=lambda: stage_C(
                1, es=[6, 7], emit_evac=False))
            stage_C_evac(6)
            stage_C_evac(7)
            wload("g", "gd", gwout, wpj_d[:, 8:16, :], pjk[8:16],
                  ["swout"])
            szb_cast(0)
            wload("q", "wd", gx0, wpj_d[:, 0:8, :], pjk[0:8], ["sxt0"])
            fc_quarter(0, [(s1, "s1", 0, 8)])
            wload("q", "wd", s1, wfc_d[:, :, 3072:4096], fck, ["s1"])
            fc_quarter(1, [(gq1a, su0k[0], 0, 4), (gq1b, su1k[0], 4, 8)])
            wload("g", "gd", gb0, wpj_d[:, 16:24, :], pjk[16:24],
                  ["bfA0"] + su0k)
            wload("g", "gd", gb1, wfc_d[:, :, 1024:2048], fck,
                  ["bfA1"] + su1k)
            stage_D_head(1, mid_hook=lambda: pj_half(
                0, gx0, "sxt0", gwout, "swout", first=True, js=[0, 1],
                emit_evac=False))
            pj_evac(0, 0, True)
            pj_evac(0, 1, True)
            pj_half(0, gx0, "sxt0", gwout, "swout", first=True,
                    js=range(2, NCH))
            fc_quarter(2, [(gwin, "swin", 0, 8)], jjs=[0, 1])
            stage_D_tail(1, 1, mid_hook=lambda: fc_quarter(
                2, [(gwin, "swin", 0, 8)], jjs=[2, 3]))
            wload("q", "wd", gx1, wpj_d[:, 24:32, :], pjk[24:32],
                  ["sxt1"])
            fc_quarter(2, [(gwin, "swin", 0, 8)], jjs=range(4, 8))
            wload("q", "wd", gx0, wpj_d[:, 0:8, :], pjk[0:8], ["sxt0"])
            wload("g", "gd", gwout, wpj_d[:, 8:16, :], pjk[8:16],
                  ["swout"])
            fc_quarter(3, [(s1, "s1", 0, 8)])
            szb_cast(1)
            pj_half(0, gb0, "bfA0", gx1, "sxt1", first=False)
            wload("q", "wd", s1, wfc_d[:, :, 0:1024], fck, ["s1"])
            wload("q", "wd", gwin, wfc_d[:, :, 2048:3072], fck,
                  ["swin"])
            fc_quarter(0, [(s1, "s1", 0, 8)])
            fc_quarter(1, [(gb1, "bfA1", 0, 8)])
            ln3_fire(r, 0, mid_hook=lambda: pj_half(
                1, gx0, "sxt0", gwout, "swout", first=True, js=[0, 1],
                emit_evac=False))
            pj_evac(1, 0, True)
            pj_evac(1, 1, True)
            wload("g", "gd", gb0, wpj_d[:, 16:24, :], pjk[16:24],
                  ["bfA0"] + su0k)
            pj_half(1, gx0, "sxt0", gwout, "swout", first=True,
                    js=range(2, NCH))
            wload("g", "gd", gb1, wpj_d[:, 24:32, :], pjk[24:32],
                  ["bfA1"] + su1k)
            fc_quarter(2, [(gwin, "swin", 0, 8)])
            wload("q", "wd", s1, wfc_d[:, :, 3072:4096], fck, ["s1"])
            fc_quarter(3, [(s1, "s1", 0, 8)])
            pj_half(1, gb0, "bfA0", gb1, "bfA1", first=False)
            if r + 1 < ROUNDS:
                t1n = (r + 1) * 2 * TT
                load_x(0, xr[:, :, t1n:t1n + TT])
                load_x(1, xr[:, :, t1n + TT:t1n + 2 * TT])
                wload("q", "wd", swin, win_d, ["wsc_wi"], ["swin"])
                for _e in range(4):
                    P.extra.setdefault(f"su0_{_e}", set()).add("bfA0")
                stage_A(0, add_beta=False, es=[0, 1], emit_evac=False)

                def _mh():
                    stage_A_evac(0, 0)
                    stage_A_evac(0, 1)
                    stage_A(0, add_beta=False, es=[2, 3], emit_cast=False,
                            emit_evac=False)
                ln3_fire(r, 1, mid_hook=_mh)
            else:
                ln3_fire(r, 1)

        P.emit("v", "tensor_scalar", sbpb[:], sv[:, C_BPJ:C_BPJ + 8],
               sscal[:, 2:3], 1.0, ALU.mult, op1=ALU.mult,
               reads=["sv", "sscal"], writes=["sbpb"])

        for r in range(ROUNDS):
            emit_mega(r)

        total_out = P.cnt["out"]

        def final_wait(eng, total_out=total_out):
            eng.wait_ge(P.sem["out"], total_out)
        P.raw("g", final_wait)

        with nc.Block() as block:
            @block.sync
            def _(eng):
                for f in P.ops["q"]:
                    f(eng)

            @block.tensor
            def _(eng):
                for f in P.ops["t"]:
                    f(eng)

            @block.vector
            def _(eng):
                for f in P.ops["v"]:
                    f(eng)

            @block.scalar
            def _(eng):
                for f in P.ops["s"]:
                    f(eng)

            @block.gpsimd
            def _(eng):
                for f in P.ops["g"]:
                    f(eng)

    return nc


def make_inputs_for_core(core, arrs):
    (x, w_in, b_in, log_a, b_p, c_p, w_out, b_out, gamma, beta,
     w_fc, b_fc, a_fc, b_fcm, w_proj, b_proj, a_proj, b_projm) = arrs
    b = core // 4
    t0 = (core % 4) * TPC
    xT = np.ascontiguousarray(x[b, t0:t0 + TPC, :].T)
    if t0 == 0:
        xhT = np.zeros((D, HALO), np.float32)
        hm = np.zeros((128, 1), np.float32)
    else:
        xhT = np.ascontiguousarray(x[b, t0 - HALO:t0, :].T)
        hm = np.ones((128, 1), np.float32)

    def packcols(*vecs):
        cols = []
        for v in vecs:
            cols.append(v.reshape(NCH, 128).T)
        return np.ascontiguousarray(np.concatenate(cols, axis=1))

    cvec = packcols(b_in, b_out, gamma, beta, b_proj, log_a.reshape(-1),
                    b_p.reshape(-1), c_p.reshape(-1))
    bfc2 = np.ascontiguousarray(b_fc.reshape(32, 128).T)
    return {
        "xT": xT, "xhT": xhT, "hmask": hm,
        "winT": np.ascontiguousarray(w_in.T),
        "woutT": np.ascontiguousarray(w_out.T),
        "wfcT": np.ascontiguousarray(w_fc.T),
        "wprojT": np.ascontiguousarray(w_proj.T),
        "afc": np.ascontiguousarray(a_fc),
        "bfcT": np.ascontiguousarray(b_fcm.T),
        "aproj": np.ascontiguousarray(a_proj),
        "bprojT": np.ascontiguousarray(b_projm.T),
        "cvec": cvec, "bfc2": bfc2,
        "ident": np.eye(128, dtype=np.float32),
        "ones1f": np.ones((1, 128), np.float32),
        "onescb": np.ones((128, 1), dtype=ml_dtypes.bfloat16),
    }


def kernel(L0_attn_x, L0_attn_W_in, L0_attn_b_in, L0_attn_log_A, L0_attn_B_p,
           L0_attn_C_p, L0_attn_W_out, L0_attn_b_out, L0_attn_gamma,
           L0_attn_beta, L0_ffn_W_fc, L0_ffn_b_fc, L0_ffn_A_fc, L0_ffn_B_fc,
           L0_ffn_W_proj, L0_ffn_b_proj, L0_ffn_A_proj, L0_ffn_B_proj):
    arrs = [np.asarray(a, dtype=np.float32) for a in (
        L0_attn_x, L0_attn_W_in, L0_attn_b_in, L0_attn_log_A, L0_attn_B_p,
        L0_attn_C_p, L0_attn_W_out, L0_attn_b_out, L0_attn_gamma, L0_attn_beta,
        L0_ffn_W_fc, L0_ffn_b_fc, L0_ffn_A_fc, L0_ffn_B_fc, L0_ffn_W_proj,
        L0_ffn_b_proj, L0_ffn_A_proj, L0_ffn_B_proj)]
    nc = build_nc()
    in_maps = [make_inputs_for_core(c, arrs) for c in range(NCORES)]
    res = run_bass_kernel_spmd(nc, in_maps, core_ids=list(range(NCORES)))
    out = np.empty((B, S, D), np.float32)
    for c in range(NCORES):
        bidx = c // 4
        t0 = (c % 4) * TPC
        out[bidx, t0:t0 + TPC, :] = np.asarray(res.results[c]["youtT"]).T
    return out


# version 68
# speedup vs baseline: 1.0156x; 1.0156x over previous
import contextlib
import numpy as np
import ml_dtypes

import concourse.bass as bass
import concourse.mybir as mybir
from concourse.bass_utils import run_bass_kernel_spmd

F32 = mybir.dt.float32
BF16 = mybir.dt.bfloat16
AF = mybir.ActivationFunctionType
ALU = mybir.AluOpType
AX = mybir.AxisListType.X

NCORES = 8
B, S, D = 2, 8192, 1024
TPC = B * S // NCORES
HALO = 512
NCH = D // 128
TT = 512
ROUNDS = 2
FF = 4096
NFCH = FF // 128
EPS = 1e-5
MAGIC = float(np.float32(1.5 * 2.0 ** 23))
INV_D = 1.0 / float(D)

C_BIN, C_BOUT, C_GAM, C_BET, C_BPJ, C_LOGA, C_BP, C_CP = 0, 8, 16, 24, 32, 40, 48, 56


class Prog:

    ENGS = ("t", "v", "s", "g", "q")
    DSTREAMS = ("xd", "qd", "wfc", "wf2", "wpj", "wp2", "wd", "gd", "sd", "out", "l0", "l1", "l2", "b0", "b1")

    def __init__(self, nc, es):
        self.nc = nc
        self.sem = {e: es.enter_context(nc.semaphore(f"sem_{e}"))
                    for e in self.ENGS + self.DSTREAMS}
        self.cnt = {k: 0 for k in self.sem}
        self.ops = {e: [] for e in self.ENGS}
        self.writer = {}
        self.readers = {}
        self.dirty = {e: set() for e in self.ENGS}
        self.last_wait = {}
        self.max_waited = {}
        self.issued_floor = {}
        self.extra = {}

    def _edge(self, e, tok, waits, need_drain):
        sem_name, val, is_dma = tok
        if is_dma and sem_name == "qd":
            val = self.cnt["qd"]
        if sem_name == e and not is_dma:
            if val > 0:
                need_drain[0] = True
            return
        key = (e, sem_name)
        if self.last_wait.get(key, -1) >= val:
            return
        waits[sem_name] = max(waits.get(sem_name, 0), val)

    def emit(self, e, method, *args, reads=(), writes=(), sem=None, delta=None,
             accum=False, **kwargs):
        waits = {}
        need_drain = [False]
        for k in reads:
            w = self.writer.get(k)
            if w is not None:
                self._edge(e, w, waits, need_drain)
        for k in writes:
            if accum:
                continue
            for ok in self.extra.pop(k, ()):
                ow = self.writer.get(ok)
                if ow is not None:
                    self._edge(e, ow, waits, need_drain)
                for r in self.readers.get(ok, ()):
                    self._edge(e, r, waits, need_drain)
            w = self.writer.get(k)
            if w is not None:
                self._edge(e, w, waits, need_drain)
            for r in self.readers.get(k, ()):
                self._edge(e, r, waits, need_drain)
        if need_drain[0]:
            dirty_hit = any(k in self.dirty[e] for k in reads) or any(
                k in self.dirty[e] for k in writes)
            for k in writes:
                for r in self.readers.get(k, ()):
                    if r[0] == e and not r[2]:
                        dirty_hit = True
            need_drain[0] = dirty_hit

        is_dma = method == "dma_start"
        if need_drain[0]:
            self.dirty[e] = set()
        d = delta if delta is not None else (16 if is_dma else 1)
        sem_name = sem if sem is not None else e
        self.cnt[sem_name] += d
        tok = (sem_name, self.cnt[sem_name], is_dma)
        for k in writes:
            self.writer[k] = tok
            self.readers[k] = []
            self.dirty[e].discard(k)
        for k in reads:
            self.readers.setdefault(k, []).append(tok)
        if not is_dma:
            for k in writes:
                self.dirty[e].add(k)

        sem_h = self.sem[sem_name]
        wait_list = [(self.sem[sn], v) for sn, v in waits.items()]
        for sn, v in waits.items():
            self.last_wait[(e, sn)] = max(self.last_wait.get((e, sn), -1), v)
            self.max_waited[sn] = max(self.max_waited.get(sn, 0), v)
        if is_dma:
            if sem_name in ("xd", "wd", "gd", "out"):
                floor = self.cnt[sem_name] - d
            else:
                floor = self.max_waited.get(sem_name, 0)
            if floor > self.issued_floor.get(sem_name, 0):
                self.issued_floor[sem_name] = floor
                wait_list.insert(0, (self.sem[sem_name], floor))
        drain = need_drain[0]

        def thunk(eng, wait_list=wait_list, drain=drain, method=method,
                  args=args, kwargs=kwargs, sem_h=sem_h, d=d):
            for s_h, v in wait_list:
                eng.wait_ge(s_h, v)
            if drain:
                eng.drain()
            ins = getattr(eng, method)(*args, **kwargs)
            ins.then_inc(sem_h, d)

        self.ops[e].append(thunk)
        return tok

    def barrier(self):
        allsems = self.ENGS + self.DSTREAMS
        snap = {o: self.cnt[o] for o in allsems}
        for e in self.ENGS:
            def thunk(eng, e=e, snap=snap):
                for o in snap:
                    if o != e and snap[o] > 0:
                        eng.wait_ge(self.sem[o], snap[o])
                eng.drain()
            self.ops[e].append(thunk)
            self.dirty[e] = set()
            for o in allsems:
                if o != e:
                    self.last_wait[(e, o)] = max(
                        self.last_wait.get((e, o), -1), snap[o])
                    self.max_waited[o] = max(self.max_waited.get(o, 0), snap[o])

    def bump_stream(self, sem_name, keys):
        end = self.cnt[sem_name]
        for k in keys:
            if k in self.writer:
                self.writer[k] = (sem_name, end, True)

    def raw(self, e, fn):
        self.ops[e].append(fn)


def build_nc():
    nc = bass.Bass("TRN2", target_bir_lowering=False, debug=False)

    dt = lambda n, shp, dty, kind: nc.dram_tensor(n, shp, dty, kind=kind).ap()
    xT = dt("xT", [D, TPC], F32, "ExternalInput")
    xhT = dt("xhT", [D, HALO], F32, "ExternalInput")
    hmask = dt("hmask", [128, 1], F32, "ExternalInput")
    winT = dt("winT", [D, D], F32, "ExternalInput")
    woutT = dt("woutT", [D, D], F32, "ExternalInput")
    wfcT = dt("wfcT", [D, FF], F32, "ExternalInput")
    wprojT = dt("wprojT", [FF, D], F32, "ExternalInput")
    afc = dt("afc", [16, D], F32, "ExternalInput")
    bfcT = dt("bfcT", [16, FF], F32, "ExternalInput")
    aproj = dt("aproj", [16, FF], F32, "ExternalInput")
    bprojT = dt("bprojT", [16, D], F32, "ExternalInput")
    cvec = dt("cvec", [128, 64], F32, "ExternalInput")
    bfc2 = dt("bfc2", [128, 32], F32, "ExternalInput")
    ident = dt("ident", [128, 128], F32, "ExternalInput")
    ones1f = dt("ones1f", [1, 128], F32, "ExternalInput")
    onescb = dt("onescb", [128, 1], BF16, "ExternalInput")
    youtT = dt("youtT", [D, TPC], F32, "ExternalOutput")
    wfce = dt("wfce", [D, FF], BF16, "Internal")
    wproje = dt("wproje", [FF, D], BF16, "Internal")
    wine = dt("wine", [D, D], BF16, "Internal")
    woute = dt("woute", [D, D], BF16, "Internal")

    es = contextlib.ExitStack()
    with es:
        P = Prog(nc, es)
        sb = lambda n, shp, dty: es.enter_context(nc.sbuf_tensor(n, shp, dty))
        ps = lambda n, shp: es.enter_context(nc.psum_tensor(n, shp, F32))

        closed_regions = []

        def _span(h):
            m = nc.lookup_mloc(h)
            import math
            nb = 1
            for d in list(h.shape)[1:]:
                nb *= int(d)
            nb *= mybir.dt.size(h.dtype)
            return int(m.addr), int(m.addr) + nb

        def aopen(h, keys):
            a, b = _span(h)
            olds = set()
            for (s, e2, ks) in closed_regions:
                if s < b and a < e2:
                    olds |= ks
            olds -= set(keys)
            if olds:
                for k in keys:
                    P.extra.setdefault(k, set()).update(olds)
            return (a, b, frozenset(keys))

        def aclose(tokens):
            closed_regions.extend(tokens)

        sv = sb("sv", [128, 64], F32)
        sbfc = sb("sbfc", [128, 32], F32)
        sid = sb("sid", [128, 128], F32)
        so1f = sb("so1f", [33, 128], F32)
        socb = sb("socb", [128, 1], BF16)
        socd = sb("socd", [128, 1], BF16)
        shm = sb("shm", [128, 1], F32)
        sdecay = sb("sdecay", [128, 8], F32)
        sBb = sb("sBb", [128, 8], F32)
        sscal = sb("sscal", [128, 8], F32)
        samax = sb("samax", [128, 33], F32)
        srow_ab = sb("srow_ab", [33, 512], F32)
        smcol = sb("smcol", [128, 8], F32)
        sbpb = sb("sbpb", [128, 8], F32)

        pA = ps("pA", [128, 512])
        pB = ps("pB", [128, 512])
        pW = [ps(f"pW{i}", [128, 512]) for i in range(4)]
        pst0 = ps("pst0", [128, 512])
        pbc = ps("pbc", [128, 512])

        col = lambda base, c: sv[:, base + c:base + c + 1]

        small_keys = []
        for dst, src, key in [
            (sv[:], cvec, "sv"), (sbfc[:], bfc2, "sbfc"), (sid[:], ident, "sid"),
            (so1f[0:1, :], ones1f, "so1f"), (so1f[32:33, :], ones1f, "so1f"),
            (socb[:], onescb, "socb"), (shm[:], hmask, "shm"),
        ]:
            P.emit("q", "dma_start", out=dst, in_=src, writes=[key], sem="qd")
            small_keys.append(key)

        P.emit("s", "activation", socd[:], socb[:], AF.Identity,
               scale=np.float32(1.0 / D), reads=["socb"], writes=["socd"])

        P.emit("s", "activation", sdecay[:], sv[:, C_LOGA:C_LOGA + 8], AF.Exp,
               reads=["sv"], writes=["sdecay"])
        P.emit("s", "activation", sdecay[:], sdecay[:], AF.Ln, bias=np.float32(1.0),
               reads=["sdecay"], writes=["sdecay"])
        P.emit("s", "activation", sdecay[:], sdecay[:], AF.Exp, scale=np.float32(-1.0),
               reads=["sdecay"], writes=["sdecay"])
        P.emit("v", "tensor_tensor", sBb[:], sv[:, C_BIN:C_BIN + 8],
               sv[:, C_BP:C_BP + 8], ALU.mult, reads=["sv"], writes=["sBb"])

        def prep_weight(wsrc, wdst, nchunks, fwidth, a_sb, a_key, bT_sb, bT_key,
                        col_recip, col_s, tag, wraw, qbuf0, qbuf1,
                        abt_full, abt_full2, chunk_hook=None, preloaded=False):
            if True:
                wbig = wraw[:].rearrange("p (c f) -> p c f", c=nchunks)
                wbig_keys = ([f"wbig_{tag}_g{g}" for g in range(nchunks // 4)]
                             + [f"wbig_{tag}{c}" for c in range(nchunks)])
                wsrc_r = wsrc.rearrange("(c p) f -> p c f", p=128)
                GRP = 2 if nchunks == NCH else 4
                ngrp = nchunks // GRP
                lstream = lambda g: ("l0", "l1", "l2")[g % 3]
                if not preloaded:
                    for g0 in range(min(3, ngrp)):
                        sl = slice(g0 * GRP, (g0 + 1) * GRP)
                        P.emit("q", "dma_start", out=wbig[:, sl, :],
                               in_=wsrc_r[:, sl, :],
                               writes=[f"wbig_{tag}_g{g0}"], sem=lstream(g0))
                    for g0 in range(ngrp):
                        for c in range(g0 * GRP, (g0 + 1) * GRP):
                            P.emit("v", "tensor_reduce", samax[:, c:c + 1],
                                   wbig[:, c, :], AX, ALU.max,
                                   apply_absolute_value=True,
                                   reads=[f"wbig_{tag}_g{g0}"],
                                   writes=["samax"])
                        if g0 + 3 < ngrp:
                            sl = slice((g0 + 3) * GRP, (g0 + 4) * GRP)
                            P.emit("q", "dma_start", out=wbig[:, sl, :],
                                   in_=wsrc_r[:, sl, :],
                                   writes=[f"wbig_{tag}_g{g0 + 3}"],
                                   sem=lstream(g0 + 3))
                P.emit("v", "tensor_reduce", samax[:, 32:33], samax[:, 0:nchunks],
                       AX, ALU.max, apply_absolute_value=True,
                       reads=["samax"], writes=["samax1"])
                P.emit("t", "transpose", pst0[0:1, 0:128], samax[:, 32:33], sid[:],
                       reads=["samax1", "sid"], writes=["pst0"])
                C_S, C_R = srow_ab[0:1, 0:1], srow_ab[0:1, 1:2]
                P.emit("v", "tensor_reduce", C_S, pst0[0:1, 0:128],
                       AX, ALU.max, reads=["pst0"], writes=["srow"])
                P.emit("v", "reciprocal", C_R, C_S,
                       reads=["srow"], writes=["srow"])
                P.emit("v", "tensor_single_scalar", C_R, C_R,
                       127.0, ALU.mult, reads=["srow"], writes=["srow"])
                P.emit("v", "tensor_single_scalar", C_S, C_S,
                       1.0 / 127.0, ALU.mult, reads=["srow"], writes=["srow"])
                P.emit("t", "matmul", pbc[:, 0:1], so1f[0:1, :],
                       C_R,
                       start=True, stop=True, reads=["so1f", "srow"], writes=["pbc"])
                P.emit("v", "tensor_copy", sscal[:, col_recip:col_recip + 1],
                       pbc[:, 0:1], reads=["pbc"], writes=["sscal"])
                P.emit("t", "matmul", pbc[:, 0:1], so1f[0:1, :],
                       C_S,
                       start=True, stop=True, reads=["so1f", "srow"], writes=["pbc"])
                P.emit("v", "tensor_copy", sscal[:, col_s:col_s + 1],
                       pbc[:, 0:1], reads=["pbc"], writes=["sscal"])
                if tag == "pj":
                    for h in range(4):
                        sl = slice(h * D, (h + 1) * D)
                        P.emit("q", "dma_start", out=a_sb[:],
                               in_=aproj[:, sl], reads=[a_key],
                               writes=[a_key], sem="qd")
                        P.emit("v", "tensor_scalar", abf[a_key][:, sl],
                               a_sb[:],
                               sscal[0:16, col_recip:col_recip + 1], 2.0,
                               ALU.mult, op1=ALU.mult,
                               reads=[a_key, "sscal"], writes=[a_key + "b"])
                else:
                    P.emit("v", "tensor_scalar", abf[a_key][:], a_sb[:],
                           sscal[0:16, col_recip:col_recip + 1], 2.0, ALU.mult,
                           op1=ALU.mult, reads=[a_key, "sscal"],
                           writes=[a_key + "b"])

                nseg = fwidth // 512
                for c in range(nchunks):
                    qbuf = (qbuf0 if c % 2 == 0 else qbuf1)[:, 0:fwidth]
                    qk = f"qbuf{c % 2}"
                    abt = (abt_full if c % 2 == 0 else abt_full2)[:, 0:fwidth]
                    abk = "abt" if c % 2 == 0 else "abt2"
                    P.emit("s", "mul", wbig[:, c, :], wbig[:, c, :],
                           sscal[:, col_recip:col_recip + 1],
                           reads=[f"wbig_{tag}_g{c // GRP}", "sscal"],
                           writes=[f"wbig_{tag}{c}"])
                    P.emit("v", "tensor_scalar", qbuf, wbig[:, c, :],
                           MAGIC, MAGIC, ALU.add, op1=ALU.subtract,
                           reads=[f"wbig_{tag}{c}"], writes=[qk])
                    for n in range(nseg):
                        ring = [(pA, "pA"), (pB, "pB"), (pW[0], "pW0"),
                                (pW[1], "pW1"), (pW[2], "pW2"), (pW[3], "pW3")]
                        pX, pk = ring[(c * nseg + n) % 6]
                        sl = slice(n * 512, (n + 1) * 512)
                        P.emit("t", "matmul", pX[:],
                               abf[a_key][:, c * 128:(c + 1) * 128],
                               abf[bT_key][:, sl], start=True, stop=True,
                               reads=[a_key + "b", bT_key + "b"], writes=[pk])
                        P.emit("s", "copy", abt[:, sl], pX[:],
                               reads=[pk], writes=[abk])
                    P.emit("v", "tensor_tensor", qbuf, abt,
                           qbuf, ALU.add,
                           reads=[abk, qk], writes=[qk])
                    P.emit("g", "dma_start",
                           out=wdst[c * 128:(c + 1) * 128, :], in_=qbuf,
                           reads=[qk], writes=[f"wdst_{tag}_{c}"],
                           sem=("wfc" if c % 2 == 0 else "wf2") if tag == "fc"
                               else ("wpj" if c % 2 == 0 else "wp2"))
                    if chunk_hook is not None:
                        chunk_hook(c)

        with nc.sbuf_tensor("safc", [16, D], F32) as safc, \
             nc.sbuf_tensor("safcb", [16, D], BF16) as safcb, \
             nc.sbuf_tensor("sbfcTb", [16, FF], BF16) as sbfcTb, \
             nc.sbuf_tensor("saprojb", [16, FF], BF16) as saprojb, \
             nc.sbuf_tensor("sbprojTb", [16, D], BF16) as sbprojTb, \
             nc.sbuf_tensor("wraw", [128, 32768], F32) as wraw, \
             nc.sbuf_tensor("qbuf0", [128, FF], BF16) as qbuf0, \
             nc.sbuf_tensor("qbuf1", [128, FF], BF16) as qbuf1, \
             nc.sbuf_tensor("abt", [128, FF], BF16) as abt_full, \
             nc.sbuf_tensor("abt2", [128, FF], BF16) as abt_full2:
            fc_keys = ([f"wbig_fc_g{g}" for g in range(4)]
                       + [f"wbig_fc{c}" for c in range(NCH)])
            pj_keys = ([f"wbig_pj_g{g}" for g in range(8)]
                       + [f"wbig_pj{c}" for c in range(NFCH)])
            _abk = [aopen(safc, ["safc"]),
                    aopen(safcb, ["safcb"]), aopen(sbfcTb, ["sbfcTb"]),
                    aopen(saprojb, ["saprojb"]), aopen(sbprojTb, ["sbprojTb"]),
                    aopen(wraw, fc_keys + pj_keys),
                    aopen(qbuf0, ["qbuf0"]), aopen(qbuf1, ["qbuf1"]),
                    aopen(abt_full, ["abt"]), aopen(abt_full2, ["abt2"])]
            P.emit("q", "dma_start", out=safc[:], in_=afc, writes=["safc"],
                   sem="qd")
            small_keys.append("safc")
            P.bump_stream("qd", small_keys)
            abf = {"safc": safcb, "sbfcT": sbfcTb,
                   "saproj": saprojb, "sbprojT": sbprojTb}
            P.emit("g", "dma_start", out=sbfcTb[:], in_=bfcT,
                   writes=["sbfcTb"], sem="b0")
            P.emit("g", "dma_start", out=sbprojTb[:], in_=bprojT,
                   writes=["sbprojTb"], sem="b1")
            with nc.sbuf_tensor("wstg", [128, 1, D], F32) as wstg, \
                 nc.sbuf_tensor("wstgb", [128, 4, D], BF16) as wstgb:
                _tk = [aopen(wstg, ["wstg"]), aopen(wstgb, ["wstgb"])]
                for wsrc, wdst, tg in [(winT, wine, "wi"),
                                       (woutT, woute, "wo")]:
                    wr = wsrc.rearrange("(c p) f -> p c f", p=128)
                    wd = wdst.rearrange("(c p) f -> p c f", p=128)
                    for h in range(2):
                        for j in range(4):
                            c = 4 * h + j
                            P.emit("g", "dma_start", out=wstg[:],
                                   in_=wr[:, c:c + 1, :],
                                   reads=["wstg"], writes=["wstg"], sem="gd")
                            P.emit("s", "activation", wstgb[:, j, :],
                                   wstg[:, 0, :], AF.Identity,
                                   reads=["wstg"], writes=["wstgb"])
                        P.emit("g", "dma_start",
                               out=wd[:, 4 * h:4 * h + 4, :],
                               in_=wstgb[:], reads=["wstgb"],
                               writes=[f"wsc_{tg}"], sem="gd")
                aclose(_tk)
            wpj_src = wprojT.rearrange("(c p) f -> p c f", p=128)
            wbig_pj = wraw[:].rearrange("p (c f) -> p c f", c=NFCH)

            def pj_reduces(g):
                for c in range(g * 4, (g + 1) * 4):
                    P.emit("v", "tensor_reduce", samax[:, c:c + 1],
                           wbig_pj[:, c, :], AX, ALU.max,
                           apply_absolute_value=True,
                           reads=[f"wbig_pj_g{g}"], writes=["samax"])

            def chase(cfc):
                g = cfc
                P.extra.setdefault(f"wbig_pj_g{g}", set()).update(
                    {f"wbig_fc{g}", f"wbig_fc_g{g // 2}"})
                sl = slice(g * 4, (g + 1) * 4)
                P.emit(("q", "q", "g")[g % 3], "dma_start",
                       out=wbig_pj[:, sl, :], in_=wpj_src[:, sl, :],
                       writes=[f"wbig_pj_g{g}"],
                       sem=("l0", "l1", "b0")[g % 3])
                if g >= 2:
                    pj_reduces(g - 2)

            prep_weight(wfcT, wfce, NCH, FF, safc, "safc", None, "sbfcT",
                        0, 1, "fc", wraw, qbuf0, qbuf1, abt_full, abt_full2,
                        chunk_hook=chase)
            pj_reduces(6)
            pj_reduces(7)
            with nc.sbuf_tensor("aptmp", [16, D], F32) as aptmp:
                _apk = [aopen(aptmp, ["saproj"])]
                prep_weight(wprojT, wproje, NFCH, D, aptmp, "saproj",
                            None, "sbprojT", 2, 3, "pj", wraw, qbuf0, qbuf1,
                            abt_full, abt_full2, preloaded=True)
                aclose(_apk)
            aclose(_abk)

        smub = sb("smub", [128, TT], F32)
        sab = sb("sab", [128, TT], F32)
        sz = sb("sz", [128, NCH, 2 * TT], F32)

        def bc3(ap2d):
            return ap2d.rearrange("p (a n) -> p a n", a=1).broadcast_to(
                [128, NCH, TT])

        def colb3(base):
            return sv[:, base:base + 8].rearrange(
                "p (c a) -> p c a", a=1).broadcast_to([128, NCH, TT])

        def decb(c):
            return sdecay[:, c:c + 1].broadcast_to([128, TT])

        def ln_stats(src_ap, src_keys, ssq, sq_key, sob, ob_key, sq_eng="s",
                     mid_hook=None):
            sq_keys = sq_key if isinstance(sq_key, list) else [sq_key]
            ob_keys = ob_key if isinstance(ob_key, list) else [ob_key]
            """Feature-LN stats of src -> broadcast tiles sab (a) and smub (mu*a).

            src_ap: [128, NCH, TT] view (possibly strided). Emits casts on s/g,
            stats matmuls on t, row math spread over s/g/v, broadcasts on t,
            evacuations on s.
            """
            if sq_eng == "s":
                P.emit("s", "activation", ssq[:], src_ap, AF.Square,
                       reads=src_keys, writes=sq_keys)
            else:
                P.emit("v", "tensor_tensor", ssq[:], src_ap, src_ap,
                       ALU.mult, reads=src_keys, writes=sq_keys)
            P.emit("v", "tensor_copy", sob[:], src_ap,
                   reads=src_keys, writes=ob_keys)
            for c in range(NCH):
                P.emit("t", "matmul", pst0[0:1, :], socd[:], sob[:, c, :],
                       start=(c == 0), stop=(c == NCH - 1), accum=(c > 0),
                       tile_position=(0, 0),
                       reads=["socd"] + ob_keys, writes=["pst0"])
                P.emit("t", "matmul", pst0[32:33, :], socd[:], ssq[:, c, :],
                       start=(c == 0), stop=(c == NCH - 1), accum=(c > 0),
                       tile_position=(0, 32),
                       reads=["socd"] + sq_keys, writes=["pst0"])
            MU, MSQ, VAR = pst0[0:1, :], pst0[32:33, :], pst0[96:97, :]
            A_, MUA = srow_ab[0:1, :], srow_ab[32:33, :]
            P.emit("s", "activation", A_, MU, AF.Square,
                   reads=["pst0"], writes=["srow"])
            P.emit("v", "tensor_tensor", VAR, MSQ, A_,
                   ALU.subtract, reads=["pst0", "srow"], writes=["pst0"])
            P.emit("s", "activation", A_, VAR, AF.Ln,
                   bias=np.float32(EPS), reads=["pst0"], writes=["srow"])
            P.emit("s", "activation", A_, A_, AF.Exp,
                   scale=np.float32(-0.5), reads=["srow"], writes=["srow"])
            P.emit("v", "tensor_tensor", MUA, MU, A_,
                   ALU.mult, reads=["pst0", "srow"], writes=["srow"])
            if mid_hook is not None:
                mid_hook()
            P.emit("t", "matmul", pbc[:], so1f[0:1, :], A_,
                   start=True, stop=True, reads=["so1f", "srow"], writes=["pbc"])
            P.emit("s", "copy", sab[:], pbc[:], reads=["pbc"], writes=["sab"])
            P.emit("t", "matmul", pbc[:], so1f[32:33, :], MUA,
                   start=True, stop=True, reads=["so1f", "srow"], writes=["pbc"])
            P.emit("s", "copy", smub[:], pbc[:], reads=["pbc"], writes=["smub"])

        sxt1 = sb("sxt1", [128, NCH, TT], F32)
        sbfA1 = sb("sbfA1", [128, NCH, TT], BF16)
        sut1 = sb("sut1", [128, NCH, TT], BF16)
        swin = sb("swin", [128, NCH, D], BF16)
        sstb = sb("sstb", [128, NCH, TT], F32)
        sxt0 = sb("sxt0", [128, NCH, TT], F32)
        sbfA0 = sb("sbfA0", [128, NCH, TT], BF16)
        sut0 = sb("sut0", [128, NCH, TT], BF16)
        swout = sb("swout", [128, NCH, D], BF16)
        sot = sb("sot", [128, NCH, TT], F32)
        sscr = sb("sscr", [128, NCH, TT], BF16)
        sG = sb("sG", [128, 16, TT], BF16)
        s1 = sb("s1", [128, NCH, 1024], BF16)
        assert nc.sbuf_base <= 229344, nc.sbuf_base

        addr = lambda h: nc.lookup_mloc(h).addr
        assert addr(sut0) == addr(sbfA0) + 8192
        assert addr(sut1) == addr(sbfA1) + 8192
        gat = nc.alloc_sbuf_tensor_at
        gwin = gat("gwin", [128, NCH, 1024], BF16, offset=addr(swin))
        gwout = gat("gwout", [128, NCH, 1024], BF16, offset=addr(swout))
        gx0 = gat("gx0", [128, NCH, 1024], BF16, offset=addr(sxt0))
        gb0 = gat("gb0", [128, NCH, 1024], BF16, offset=addr(sbfA0))
        gx1 = gat("gx1", [128, NCH, 1024], BF16, offset=addr(sxt1))
        gb1 = gat("gb1", [128, NCH, 1024], BF16, offset=addr(sbfA1))
        gq1a = gat("gq1a", [128, 4, 1024], BF16, offset=addr(sut0))
        gq1b = gat("gq1b", [128, 4, 1024], BF16, offset=addr(sut1))
        gsq = gat("gsq", [128, NCH, TT], BF16, offset=addr(sstb))
        gob = gat("gob", [128, NCH, TT], BF16, offset=addr(sstb) + 8192)
        gaf = gat("gaf", [128, NCH, TT], F32, offset=addr(sot))

        sxt2, sbfA2, sut2 = [sxt0, sxt1], [sbfA0, sbfA1], [sut0, sut1]
        su0k = [f"su0_{e}" for e in range(NCH)]
        su1k = [f"su1_{e}" for e in range(NCH)]
        sstk = [f"sst{c}" for c in range(NCH)]
        afk = [f"af{j}" for j in range(NCH)]
        fck = [f"wdst_fc_{c}" for c in range(NCH)]
        pjk = [f"wdst_pj_{c}" for c in range(NFCH)]
        _pk = [aopen(swin, ["swin"]), aopen(swout, ["swout"]),
               aopen(sxt0, ["sxt0"]), aopen(sxt1, ["sxt1"]),
               aopen(sbfA0, ["bfA0"]), aopen(sbfA1, ["bfA1"]),
               aopen(sut0, su0k), aopen(sut1, su1k),
               aopen(sstb, sstk), aopen(sot, ["sot"]),
               aopen(sscr, ["sscr0", "sscr1"]), aopen(sG, ["sG"]),
               aopen(s1, ["s1"]), aopen(smub, ["smub"]),
               aopen(sab, ["sab"]),
               aopen(sz, [f"sz{i}_{c}" for i in range(2) for c in range(NCH)])]

        xr = xT.rearrange("(c p) n -> p c n", p=128)
        wfc_d = wfce.rearrange("(c p) f -> p c f", p=128)
        wpj_d = wproje.rearrange("(c p) f -> p c f", p=128)
        win_d = wine.rearrange("(c p) f -> p c f", p=128)
        wout_d = woute.rearrange("(c p) f -> p c f", p=128)
        yout_d = youtT.rearrange("(c p) n -> p c n", p=128)

        def load_x(n, src_ap):
            p = n % 2
            P.emit("q", "dma_start", out=sxt2[p][:], in_=src_ap,
                   reads=[f"sxt{p}", f"bfA{p}"], writes=[f"sxt{p}"], sem="xd")

        def stage_A_evac(p, e):
            pX, pk = (pA, "pA") if e % 2 == 0 else (pB, "pB")
            P.emit("s", "activation", sut2[p][:, e, :], pX[:],
                   AF.Identity, bias=sBb[:, e:e + 1], scale=col(C_BP, e),
                   reads=[pk, "sBb", "sv"], writes=[f"su{p}_{e}"])

        def stage_A(n, add_beta=True, es=None, emit_cast=True,
                    emit_evac=True):
            p = n % 2
            xk, bk = f"sxt{p}", f"bfA{p}"
            if emit_cast:
                P.emit("s", "activation", sbfA2[p][:], sxt2[p][:],
                       AF.Identity, reads=[xk], writes=[bk])
            if add_beta:
                for c in range(NCH):
                    P.emit("s", "activation", sxt2[p][:, c, :],
                           sxt2[p][:, c, :], AF.Identity,
                           bias=col(C_BET, c),
                           reads=[xk, "sv"], writes=[xk])
            for e in (range(NCH) if es is None else es):
                pX, pk = (pA, "pA") if e % 2 == 0 else (pB, "pB")
                for k in range(NCH):
                    P.emit("t", "matmul", pX[:],
                           swin[:, k, e * 128:(e + 1) * 128],
                           sbfA2[p][:, k, :],
                           start=(k == 0), stop=(k == NCH - 1),
                           accum=(k > 0), reads=["swin", bk], writes=[pk])
                if emit_evac:
                    stage_A_evac(p, e)

        def stage_B(n):
            p = n % 2
            for c in range(NCH):
                P.emit("v", "tensor_tensor_scan", sstb[:, c, :],
                       sdecay[:, c:c + 1].broadcast_to([128, TT]),
                       sut2[p][:, c, :], smcol[:, c:c + 1],
                       ALU.mult, ALU.add,
                       reads=["sdecay", f"su{p}_{c}", f"sst{c}", f"smcol{c}"],
                       writes=[f"sst{c}"])
            P.emit("v", "tensor_copy", smcol[:, 0:NCH],
                   sstb[:, :, TT - 1:TT].rearrange("p c a -> p (c a)"),
                   reads=sstk,
                   writes=[f"smcol{c}" for c in range(NCH)])

        def stage_y(n):
            p = n % 2
            P.emit("v", "tensor_tensor", sbfA2[p][:], sstb[:], colb3(C_CP),
                   ALU.mult, reads=sstk + ["sv"], writes=[f"bfA{p}"])

        def stage_C_evac(e):
            i = e % 4
            P.emit("s", "activation", sot[:, e, :], pW[i][:],
                   AF.Identity, bias=col(C_BOUT, e),
                   reads=[f"pW{i}", "sv"], writes=["sot"])

        def stage_C(n, es=None, emit_evac=True):
            p = n % 2
            bk = f"bfA{p}"
            for e in (range(NCH) if es is None else es):
                i = e % 4
                for k in range(NCH):
                    P.emit("t", "matmul", pW[i][:],
                           swout[:, k, e * 128:(e + 1) * 128],
                           sbfA2[p][:, k, :],
                           start=(k == 0), stop=(k == NCH - 1),
                           accum=(k > 0), reads=["swout", bk], writes=[f"pW{i}"])
                if emit_evac:
                    stage_C_evac(e)

        def stage_D_head(p, mid_hook=None):
            ln_stats(sot[:], ["sot"], gsq, sstk[0:4], gob[:], sstk[4:8],
                     mid_hook=mid_hook)
            for c in range(NCH):
                P.emit("v", "scalar_tensor_tensor", sot[:, c, :],
                       sot[:, c, :], col(C_GAM, c), sab[:],
                       ALU.mult, ALU.mult,
                       reads=["sot", "sv", "sab"], writes=["sot"])
            for c in range(NCH):
                P.emit("v", "scalar_tensor_tensor", sstb[:, c, :],
                       smub[:], col(C_GAM, c), sxt2[p][:, c, :],
                       ALU.mult, ALU.subtract,
                       reads=["smub", "sv", f"sxt{p}"] + sstk,
                       writes=[f"sst{c}"])
            P.emit("v", "tensor_tensor", sxt2[p][:], sot[:], sstb[:],
                   ALU.subtract, reads=["sot"] + sstk, writes=[f"sxt{p}"])

        def stage_D_tail(p, it, mid_hook=None):
            zsl = slice(it * TT, (it + 1) * TT)
            zkeys = [f"sz{it}_{c}" for c in range(NCH)]
            ln_stats(sxt2[p][:], [f"sxt{p}"], gsq, sstk[0:4], gob[:],
                     sstk[4:8], sq_eng="v", mid_hook=mid_hook)
            for h in range(2):
                cs = slice(4 * h, 4 * h + 4)
                hb = lambda ap2d: ap2d.rearrange(
                    "p (a n) -> p a n", a=1).broadcast_to([128, 4, TT])
                P.emit("v", "tensor_tensor", sz[:, cs, zsl], sxt2[p][:, cs, :],
                       hb(sab[:]), ALU.mult,
                       reads=[f"sxt{p}", "sab"], writes=zkeys[4 * h:4 * h + 4])
                P.emit("v", "tensor_tensor", sz[:, cs, zsl], sz[:, cs, zsl],
                       hb(smub[:]), ALU.subtract,
                       reads=zkeys[4 * h:4 * h + 4] + ["smub"],
                       writes=zkeys[4 * h:4 * h + 4])

        FCRING = [(pA, "pA"), (pB, "pB"), (pW[0], "pW0"), (pW[1], "pW1")]
        PJRING = [(pW[2], "pW2"), (pW[3], "pW3"), (pA, "pA"), (pB, "pB")]

        def fc_quarter(q, pieces, jjs=None):
            for jj in (range(8) if jjs is None else jjs):
                m = 8 * q + jj
                pX, pk = FCRING[m % 4]
                for k in range(NCH):
                    sl, ky, klo = next((s, y, lo) for (s, y, lo, hi) in pieces
                                       if lo <= k < hi)
                    P.emit("t", "matmul", pX[:],
                           sl[:, k - klo, jj * 128:(jj + 1) * 128],
                           sscr[:, k, :],
                           start=(k == 0), stop=(k == NCH - 1),
                           accum=(k > 0),
                           reads=[ky, f"sscr{k // 4}"], writes=[pk])
                P.emit("s", "activation", sG[:, m % 16, :], pX[:], AF.Gelu,
                       bias=sbfc[:, m:m + 1], scale=sscal[:, 1:2],
                       reads=[pk, "sbfc", "sscal"], writes=["sG"])

        def pj_half(sub, slotL, keyL, slotH, keyH, first, js=None,
                    emit_evac=True):
            zsl = slice(sub * TT, (sub + 1) * TT)
            for j in (range(NCH) if js is None else js):
                pX, pk = PJRING[j % 4]
                korder = (list(range(8, 16)) + list(range(8))) if first \
                    else list(range(16))
                for idx, k in enumerate(korder):
                    sl, ky = (slotL, keyL) if k < 8 else (slotH, keyH)
                    P.emit("t", "matmul", pX[:],
                           sl[:, k % 8, j * 128:(j + 1) * 128], sG[:, k, :],
                           start=(idx == 0), stop=(idx == 15),
                           accum=(idx > 0), reads=[ky, "sG"], writes=[pk])
                if emit_evac:
                    pj_evac(sub, j, first)

        def pj_evac(sub, j, first):
            zsl = slice(sub * TT, (sub + 1) * TT)
            pX, pk = PJRING[j % 4]
            if first:
                P.emit("s", "activation", gaf[:, j, :], pX[:], AF.Identity,
                       bias=sbpb[:, j:j + 1],
                       reads=[pk, "sbpb"], writes=[afk[j]])
            else:
                P.emit("v", "tensor_tensor", gaf[:, j, :], pX[:],
                       gaf[:, j, :], ALU.add,
                       reads=[pk, afk[j]], writes=[afk[j]])
                P.emit("v", "scalar_tensor_tensor", sz[:, j, zsl],
                       gaf[:, j, :], sscal[:, 3:4], sz[:, j, zsl],
                       ALU.mult, ALU.add,
                       reads=[afk[j], "sscal", f"sz{sub}_{j}"],
                       writes=[f"sz{sub}_{j}"])

        def szb_cast(sub):
            zsl = slice(sub * TT, (sub + 1) * TT)
            for h in range(2):
                cs = slice(4 * h, 4 * h + 4)
                P.emit("s", "activation", sscr[:, cs, :], sz[:, cs, zsl],
                       AF.Identity,
                       reads=[f"sz{sub}_{c}" for c in range(4 * h, 4 * h + 4)]
                       + [f"sscr{h}"],
                       writes=[f"sscr{h}"])

        def prebias(sub):
            zsl = slice(sub * TT, (sub + 1) * TT)
            for c in range(NCH):
                P.emit("s", "activation", sz[:, c, zsl], sz[:, c, zsl],
                       AF.Identity, bias=col(C_BPJ, c),
                       reads=[f"sz{sub}_{c}", "sv", "sscr0", "sscr1"],
                       writes=[f"sz{sub}_{c}"])

        def ln3_fire(r, sub, mid_hook=None):
            zsl = slice(sub * TT, (sub + 1) * TT)
            zkeys = [f"sz{sub}_{c}" for c in range(NCH)]
            toff = r * 2 * TT + sub * TT
            ln_stats(sz[:, :, zsl], zkeys, gsq, sstk[0:4], gob[:], sstk[4:8],
                     mid_hook=mid_hook)
            for h in range(2):
                cs = slice(4 * h, 4 * h + 4)
                hb = lambda ap2d: ap2d.rearrange(
                    "p (a n) -> p a n", a=1).broadcast_to([128, 4, TT])
                hk = afk[4 * h:4 * h + 4]
                P.emit("v", "tensor_tensor", gaf[:, cs, :], sz[:, cs, zsl],
                       hb(sab[:]), ALU.mult,
                       reads=zkeys[4 * h:4 * h + 4] + ["sab"], writes=hk)
                P.emit("v", "tensor_tensor", gaf[:, cs, :], gaf[:, cs, :],
                       hb(smub[:]), ALU.subtract,
                       reads=hk + ["smub"], writes=hk)
                P.emit("g", "dma_start", out=yout_d[:, cs, toff:toff + TT],
                       in_=gaf[:, cs, :], reads=hk,
                       writes=[f"yout{r}{sub}{h}"], sem="out")

        def wload(eng, sem, dst, src, wkeys, hkeys):
            P.emit(eng, "dma_start", out=dst[:], in_=src,
                   reads=wkeys + hkeys, writes=hkeys, sem=sem)

        def emit_mega(r):
            for j in range(NCH):
                P.extra.setdefault(afk[j], set()).add("sot")
            P.extra.setdefault("sot", set()).update(afk)
            for e in range(NCH):
                P.extra.setdefault(f"su0_{e}", set()).add("bfA0")
                P.extra.setdefault(f"su1_{e}", set()).add("bfA1")

            t0 = r * 2 * TT
            ldq = "s" if r == 0 else "q"
            ldg = "s" if r == 0 else "g"

            def load_x0(n, src_ap):
                p = n % 2
                P.emit(ldq, "dma_start", out=sxt2[p][:], in_=src_ap,
                       reads=[f"sxt{p}", f"bfA{p}"], writes=[f"sxt{p}"],
                       sem="xd")

            if r == 0:
                load_x0(1, xhT.rearrange("(c p) n -> p c n", p=128))
                load_x0(0, xr[:, :, t0:t0 + TT])
                wload(ldq, "wd", swin, win_d, ["wsc_wi"], ["swin"])
            if r > 0:
                load_x0(1, xr[:, :, t0 + TT:t0 + 2 * TT])
                wload("g", "gd", swout, wout_d, ["wsc_wo"], ["swout"])
            if r == 0:
                stage_A(-1, add_beta=False)
                for c in range(NCH):
                    P.emit("v", "tensor_tensor_scan", sstb[:, c, :],
                           sdecay[:, c:c + 1].broadcast_to([128, TT]),
                           sut1[:, c, :], 0.0, ALU.mult, ALU.add,
                           reads=["sdecay", f"su1_{c}", f"sst{c}"],
                           writes=[f"sst{c}"])
                for c in range(NCH):
                    P.emit("v", "tensor_tensor", smcol[:, c:c + 1],
                           sstb[:, c, HALO - 1:HALO], shm[:], ALU.mult,
                           reads=[f"sst{c}", "shm"], writes=[f"smcol{c}"])
                load_x0(1, xr[:, :, t0 + TT:t0 + 2 * TT])
            if r == 0:
                stage_A(0)
            else:
                for _e in (1, 2):
                    stage_A_evac(0, _e)
                stage_A(0, emit_cast=False, es=range(3, NCH))
            stage_A(1)
            if r == 0:
                wload(ldg, "wd", swout, wout_d, ["wsc_wo"], ["swout"])
            wload("q", "wd", s1, wfc_d[:, :, 0:1024], fck, ["s1"])
            wload("q", "wd", gwin, wfc_d[:, :, 2048:3072], fck, ["swin"])
            stage_B(0)
            wload("g", "gd", gq1a, wfc_d[:, 0:4, 1024:2048], fck, su0k)
            stage_y(0)
            stage_C(0)
            stage_B(1)
            wload("g", "gd", gq1b, wfc_d[:, 4:8, 1024:2048], fck, su1k)
            stage_y(1)
            stage_D_head(0, mid_hook=lambda: stage_C(
                1, es=[0, 1], emit_evac=False))
            stage_C_evac(0)
            stage_C_evac(1)
            stage_C(1, es=range(2, 6))
            stage_D_tail(0, 0, mid_hook=lambda: stage_C(
                1, es=[6, 7], emit_evac=False))
            stage_C_evac(6)
            stage_C_evac(7)
            wload("g", "gd", gwout, wpj_d[:, 8:16, :], pjk[8:16],
                  ["swout"])
            szb_cast(0)
            wload("q", "wd", gx0, wpj_d[:, 0:8, :], pjk[0:8], ["sxt0"])
            fc_quarter(0, [(s1, "s1", 0, 8)])
            wload("q", "wd", s1, wfc_d[:, :, 3072:4096], fck, ["s1"])
            fc_quarter(1, [(gq1a, su0k[0], 0, 4), (gq1b, su1k[0], 4, 8)])
            wload("g", "gd", gb0, wpj_d[:, 16:24, :], pjk[16:24],
                  ["bfA0"] + su0k)
            wload("g", "gd", gb1, wfc_d[:, :, 1024:2048], fck,
                  ["bfA1"] + su1k)
            stage_D_head(1, mid_hook=lambda: pj_half(
                0, gx0, "sxt0", gwout, "swout", first=True, js=[0, 1],
                emit_evac=False))
            pj_evac(0, 0, True)
            pj_evac(0, 1, True)
            pj_half(0, gx0, "sxt0", gwout, "swout", first=True,
                    js=range(2, NCH))
            fc_quarter(2, [(gwin, "swin", 0, 8)], jjs=[0, 1])
            stage_D_tail(1, 1, mid_hook=lambda: fc_quarter(
                2, [(gwin, "swin", 0, 8)], jjs=[2, 3]))
            wload("q", "wd", gx1, wpj_d[:, 24:32, :], pjk[24:32],
                  ["sxt1"])
            fc_quarter(2, [(gwin, "swin", 0, 8)], jjs=range(4, 8))
            wload("q", "wd", gx0, wpj_d[:, 0:8, :], pjk[0:8], ["sxt0"])
            wload("g", "gd", gwout, wpj_d[:, 8:16, :], pjk[8:16],
                  ["swout"])
            fc_quarter(3, [(s1, "s1", 0, 8)])
            szb_cast(1)
            pj_half(0, gb0, "bfA0", gx1, "sxt1", first=False)
            wload("q", "wd", s1, wfc_d[:, :, 0:1024], fck, ["s1"])
            wload("q", "wd", gwin, wfc_d[:, :, 2048:3072], fck,
                  ["swin"])
            fc_quarter(0, [(s1, "s1", 0, 8)])
            fc_quarter(1, [(gb1, "bfA1", 0, 8)])
            ln3_fire(r, 0, mid_hook=lambda: pj_half(
                1, gx0, "sxt0", gwout, "swout", first=True, js=[0, 1],
                emit_evac=False))
            pj_evac(1, 0, True)
            pj_evac(1, 1, True)
            wload("g", "gd", gb0, wpj_d[:, 16:24, :], pjk[16:24],
                  ["bfA0"] + su0k)
            pj_half(1, gx0, "sxt0", gwout, "swout", first=True,
                    js=range(2, NCH))
            wload("g", "gd", gb1, wpj_d[:, 24:32, :], pjk[24:32],
                  ["bfA1"] + su1k)
            fc_quarter(2, [(gwin, "swin", 0, 8)])
            wload("q", "wd", s1, wfc_d[:, :, 3072:4096], fck, ["s1"])
            fc_quarter(3, [(s1, "s1", 0, 8)])
            pj_half(1, gb0, "bfA0", gb1, "bfA1", first=False)
            if r + 1 < ROUNDS:
                t1n = (r + 1) * 2 * TT
                load_x(0, xr[:, :, t1n:t1n + TT])
                wload("q", "wd", swin, win_d, ["wsc_wi"], ["swin"])
                stage_A(0, add_beta=False, es=[0], emit_evac=False)

                def _mh():
                    stage_A(0, add_beta=False, es=[1], emit_cast=False,
                            emit_evac=False)
                    stage_A_evac(0, 0)
                    stage_A(0, add_beta=False, es=[2], emit_cast=False,
                            emit_evac=False)
                ln3_fire(r, 1, mid_hook=_mh)
            else:
                ln3_fire(r, 1)

        P.emit("v", "tensor_scalar", sbpb[:], sv[:, C_BPJ:C_BPJ + 8],
               sscal[:, 2:3], 1.0, ALU.mult, op1=ALU.mult,
               reads=["sv", "sscal"], writes=["sbpb"])

        for r in range(ROUNDS):
            emit_mega(r)

        total_out = P.cnt["out"]

        def final_wait(eng, total_out=total_out):
            eng.wait_ge(P.sem["out"], total_out)
        P.raw("g", final_wait)

        with nc.Block() as block:
            @block.sync
            def _(eng):
                for f in P.ops["q"]:
                    f(eng)

            @block.tensor
            def _(eng):
                for f in P.ops["t"]:
                    f(eng)

            @block.vector
            def _(eng):
                for f in P.ops["v"]:
                    f(eng)

            @block.scalar
            def _(eng):
                for f in P.ops["s"]:
                    f(eng)

            @block.gpsimd
            def _(eng):
                for f in P.ops["g"]:
                    f(eng)

    return nc


def make_inputs_for_core(core, arrs):
    (x, w_in, b_in, log_a, b_p, c_p, w_out, b_out, gamma, beta,
     w_fc, b_fc, a_fc, b_fcm, w_proj, b_proj, a_proj, b_projm) = arrs
    b = core // 4
    t0 = (core % 4) * TPC
    xT = np.ascontiguousarray(x[b, t0:t0 + TPC, :].T)
    if t0 == 0:
        xhT = np.zeros((D, HALO), np.float32)
        hm = np.zeros((128, 1), np.float32)
    else:
        xhT = np.ascontiguousarray(x[b, t0 - HALO:t0, :].T)
        hm = np.ones((128, 1), np.float32)

    def packcols(*vecs):
        cols = []
        for v in vecs:
            cols.append(v.reshape(NCH, 128).T)
        return np.ascontiguousarray(np.concatenate(cols, axis=1))

    cvec = packcols(b_in, b_out, gamma, beta, b_proj, log_a.reshape(-1),
                    b_p.reshape(-1), c_p.reshape(-1))
    bfc2 = np.ascontiguousarray(b_fc.reshape(32, 128).T)
    return {
        "xT": xT, "xhT": xhT, "hmask": hm,
        "winT": np.ascontiguousarray(w_in.T),
        "woutT": np.ascontiguousarray(w_out.T),
        "wfcT": np.ascontiguousarray(w_fc.T),
        "wprojT": np.ascontiguousarray(w_proj.T),
        "afc": np.ascontiguousarray(a_fc),
        "bfcT": np.ascontiguousarray(b_fcm.T),
        "aproj": np.ascontiguousarray(a_proj),
        "bprojT": np.ascontiguousarray(b_projm.T),
        "cvec": cvec, "bfc2": bfc2,
        "ident": np.eye(128, dtype=np.float32),
        "ones1f": np.ones((1, 128), np.float32),
        "onescb": np.ones((128, 1), dtype=ml_dtypes.bfloat16),
    }


def kernel(L0_attn_x, L0_attn_W_in, L0_attn_b_in, L0_attn_log_A, L0_attn_B_p,
           L0_attn_C_p, L0_attn_W_out, L0_attn_b_out, L0_attn_gamma,
           L0_attn_beta, L0_ffn_W_fc, L0_ffn_b_fc, L0_ffn_A_fc, L0_ffn_B_fc,
           L0_ffn_W_proj, L0_ffn_b_proj, L0_ffn_A_proj, L0_ffn_B_proj):
    arrs = [np.asarray(a, dtype=np.float32) for a in (
        L0_attn_x, L0_attn_W_in, L0_attn_b_in, L0_attn_log_A, L0_attn_B_p,
        L0_attn_C_p, L0_attn_W_out, L0_attn_b_out, L0_attn_gamma, L0_attn_beta,
        L0_ffn_W_fc, L0_ffn_b_fc, L0_ffn_A_fc, L0_ffn_B_fc, L0_ffn_W_proj,
        L0_ffn_b_proj, L0_ffn_A_proj, L0_ffn_B_proj)]
    nc = build_nc()
    in_maps = [make_inputs_for_core(c, arrs) for c in range(NCORES)]
    res = run_bass_kernel_spmd(nc, in_maps, core_ids=list(range(NCORES)))
    out = np.empty((B, S, D), np.float32)
    for c in range(NCORES):
        bidx = c // 4
        t0 = (c % 4) * TPC
        out[bidx, t0:t0 + TPC, :] = np.asarray(res.results[c]["youtT"]).T
    return out


# version 69 (final)
# speedup vs baseline: 1.0244x; 1.0087x over previous
import contextlib
import numpy as np
import ml_dtypes

import concourse.bass as bass
import concourse.mybir as mybir
from concourse.bass_utils import run_bass_kernel_spmd

F32 = mybir.dt.float32
BF16 = mybir.dt.bfloat16
AF = mybir.ActivationFunctionType
ALU = mybir.AluOpType
AX = mybir.AxisListType.X

NCORES = 8
B, S, D = 2, 8192, 1024
TPC = B * S // NCORES
HALO = 512
NCH = D // 128
TT = 512
ROUNDS = 2
FF = 4096
NFCH = FF // 128
EPS = 1e-5
MAGIC = float(np.float32(1.5 * 2.0 ** 23))
INV_D = 1.0 / float(D)

C_BIN, C_BOUT, C_GAM, C_BET, C_BPJ, C_LOGA, C_BP, C_CP = 0, 8, 16, 24, 32, 40, 48, 56


class Prog:

    ENGS = ("t", "v", "s", "g", "q")
    DSTREAMS = ("xd", "qd", "wfc", "wf2", "wpj", "wp2", "wd", "gd", "sd", "out", "l0", "l1", "l2", "b0", "b1")

    def __init__(self, nc, es):
        self.nc = nc
        self.sem = {e: es.enter_context(nc.semaphore(f"sem_{e}"))
                    for e in self.ENGS + self.DSTREAMS}
        self.cnt = {k: 0 for k in self.sem}
        self.ops = {e: [] for e in self.ENGS}
        self.writer = {}
        self.readers = {}
        self.dirty = {e: set() for e in self.ENGS}
        self.last_wait = {}
        self.max_waited = {}
        self.issued_floor = {}
        self.extra = {}

    def _edge(self, e, tok, waits, need_drain):
        sem_name, val, is_dma = tok
        if is_dma and sem_name == "qd":
            val = self.cnt["qd"]
        if sem_name == e and not is_dma:
            if val > 0:
                need_drain[0] = True
            return
        key = (e, sem_name)
        if self.last_wait.get(key, -1) >= val:
            return
        waits[sem_name] = max(waits.get(sem_name, 0), val)

    def emit(self, e, method, *args, reads=(), writes=(), sem=None, delta=None,
             accum=False, **kwargs):
        waits = {}
        need_drain = [False]
        for k in reads:
            w = self.writer.get(k)
            if w is not None:
                self._edge(e, w, waits, need_drain)
        for k in writes:
            if accum:
                continue
            for ok in self.extra.pop(k, ()):
                ow = self.writer.get(ok)
                if ow is not None:
                    self._edge(e, ow, waits, need_drain)
                for r in self.readers.get(ok, ()):
                    self._edge(e, r, waits, need_drain)
            w = self.writer.get(k)
            if w is not None:
                self._edge(e, w, waits, need_drain)
            for r in self.readers.get(k, ()):
                self._edge(e, r, waits, need_drain)
        if need_drain[0]:
            dirty_hit = any(k in self.dirty[e] for k in reads) or any(
                k in self.dirty[e] for k in writes)
            for k in writes:
                for r in self.readers.get(k, ()):
                    if r[0] == e and not r[2]:
                        dirty_hit = True
            need_drain[0] = dirty_hit

        is_dma = method == "dma_start"
        if need_drain[0]:
            self.dirty[e] = set()
        d = delta if delta is not None else (16 if is_dma else 1)
        sem_name = sem if sem is not None else e
        self.cnt[sem_name] += d
        tok = (sem_name, self.cnt[sem_name], is_dma)
        for k in writes:
            self.writer[k] = tok
            self.readers[k] = []
            self.dirty[e].discard(k)
        for k in reads:
            self.readers.setdefault(k, []).append(tok)
        if not is_dma:
            for k in writes:
                self.dirty[e].add(k)

        sem_h = self.sem[sem_name]
        wait_list = [(self.sem[sn], v) for sn, v in waits.items()]
        for sn, v in waits.items():
            self.last_wait[(e, sn)] = max(self.last_wait.get((e, sn), -1), v)
            self.max_waited[sn] = max(self.max_waited.get(sn, 0), v)
        if is_dma:
            if sem_name in ("xd", "wd", "gd", "out"):
                floor = self.cnt[sem_name] - d
            else:
                floor = self.max_waited.get(sem_name, 0)
            if floor > self.issued_floor.get(sem_name, 0):
                self.issued_floor[sem_name] = floor
                wait_list.insert(0, (self.sem[sem_name], floor))
        drain = need_drain[0]

        def thunk(eng, wait_list=wait_list, drain=drain, method=method,
                  args=args, kwargs=kwargs, sem_h=sem_h, d=d):
            for s_h, v in wait_list:
                eng.wait_ge(s_h, v)
            if drain:
                eng.drain()
            ins = getattr(eng, method)(*args, **kwargs)
            ins.then_inc(sem_h, d)

        self.ops[e].append(thunk)
        return tok

    def barrier(self):
        allsems = self.ENGS + self.DSTREAMS
        snap = {o: self.cnt[o] for o in allsems}
        for e in self.ENGS:
            def thunk(eng, e=e, snap=snap):
                for o in snap:
                    if o != e and snap[o] > 0:
                        eng.wait_ge(self.sem[o], snap[o])
                eng.drain()
            self.ops[e].append(thunk)
            self.dirty[e] = set()
            for o in allsems:
                if o != e:
                    self.last_wait[(e, o)] = max(
                        self.last_wait.get((e, o), -1), snap[o])
                    self.max_waited[o] = max(self.max_waited.get(o, 0), snap[o])

    def bump_stream(self, sem_name, keys):
        end = self.cnt[sem_name]
        for k in keys:
            if k in self.writer:
                self.writer[k] = (sem_name, end, True)

    def raw(self, e, fn):
        self.ops[e].append(fn)


def build_nc():
    nc = bass.Bass("TRN2", target_bir_lowering=False, debug=False)

    dt = lambda n, shp, dty, kind: nc.dram_tensor(n, shp, dty, kind=kind).ap()
    xT = dt("xT", [D, TPC], F32, "ExternalInput")
    xhT = dt("xhT", [D, HALO], F32, "ExternalInput")
    hmask = dt("hmask", [128, 1], F32, "ExternalInput")
    winT = dt("winT", [D, D], F32, "ExternalInput")
    woutT = dt("woutT", [D, D], F32, "ExternalInput")
    wfcT = dt("wfcT", [D, FF], F32, "ExternalInput")
    wprojT = dt("wprojT", [FF, D], F32, "ExternalInput")
    afc = dt("afc", [16, D], F32, "ExternalInput")
    bfcT = dt("bfcT", [16, FF], F32, "ExternalInput")
    aproj = dt("aproj", [16, FF], F32, "ExternalInput")
    bprojT = dt("bprojT", [16, D], F32, "ExternalInput")
    cvec = dt("cvec", [128, 64], F32, "ExternalInput")
    bfc2 = dt("bfc2", [128, 32], F32, "ExternalInput")
    ident = dt("ident", [128, 128], F32, "ExternalInput")
    ones1f = dt("ones1f", [1, 128], F32, "ExternalInput")
    onescb = dt("onescb", [128, 1], BF16, "ExternalInput")
    youtT = dt("youtT", [D, TPC], F32, "ExternalOutput")
    wfce = dt("wfce", [D, FF], BF16, "Internal")
    wproje = dt("wproje", [FF, D], BF16, "Internal")
    wine = dt("wine", [D, D], BF16, "Internal")
    woute = dt("woute", [D, D], BF16, "Internal")

    es = contextlib.ExitStack()
    with es:
        P = Prog(nc, es)
        sb = lambda n, shp, dty: es.enter_context(nc.sbuf_tensor(n, shp, dty))
        ps = lambda n, shp: es.enter_context(nc.psum_tensor(n, shp, F32))

        closed_regions = []

        def _span(h):
            m = nc.lookup_mloc(h)
            import math
            nb = 1
            for d in list(h.shape)[1:]:
                nb *= int(d)
            nb *= mybir.dt.size(h.dtype)
            return int(m.addr), int(m.addr) + nb

        def aopen(h, keys):
            a, b = _span(h)
            olds = set()
            for (s, e2, ks) in closed_regions:
                if s < b and a < e2:
                    olds |= ks
            olds -= set(keys)
            if olds:
                for k in keys:
                    P.extra.setdefault(k, set()).update(olds)
            return (a, b, frozenset(keys))

        def aclose(tokens):
            closed_regions.extend(tokens)

        sv = sb("sv", [128, 64], F32)
        sbfc = sb("sbfc", [128, 32], F32)
        sid = sb("sid", [128, 128], F32)
        so1f = sb("so1f", [33, 128], F32)
        socb = sb("socb", [128, 1], BF16)
        socd = sb("socd", [128, 1], BF16)
        shm = sb("shm", [128, 1], F32)
        sdecay = sb("sdecay", [128, 8], F32)
        sBb = sb("sBb", [128, 8], F32)
        sscal = sb("sscal", [128, 8], F32)
        samax = sb("samax", [128, 33], F32)
        srow_ab = sb("srow_ab", [33, 512], F32)
        smcol = sb("smcol", [128, 8], F32)
        sbpb = sb("sbpb", [128, 8], F32)

        pA = ps("pA", [128, 512])
        pB = ps("pB", [128, 512])
        pW = [ps(f"pW{i}", [128, 512]) for i in range(4)]
        pst0 = ps("pst0", [128, 512])
        pbc = ps("pbc", [128, 512])

        col = lambda base, c: sv[:, base + c:base + c + 1]

        small_keys = []
        for dst, src, key in [
            (sv[:], cvec, "sv"), (sbfc[:], bfc2, "sbfc"), (sid[:], ident, "sid"),
            (so1f[0:1, :], ones1f, "so1f"), (so1f[32:33, :], ones1f, "so1f"),
            (socb[:], onescb, "socb"), (shm[:], hmask, "shm"),
        ]:
            P.emit("q", "dma_start", out=dst, in_=src, writes=[key], sem="qd")
            small_keys.append(key)

        P.emit("s", "activation", socd[:], socb[:], AF.Identity,
               scale=np.float32(1.0 / D), reads=["socb"], writes=["socd"])

        P.emit("s", "activation", sdecay[:], sv[:, C_LOGA:C_LOGA + 8], AF.Exp,
               reads=["sv"], writes=["sdecay"])
        P.emit("s", "activation", sdecay[:], sdecay[:], AF.Ln, bias=np.float32(1.0),
               reads=["sdecay"], writes=["sdecay"])
        P.emit("s", "activation", sdecay[:], sdecay[:], AF.Exp, scale=np.float32(-1.0),
               reads=["sdecay"], writes=["sdecay"])
        P.emit("v", "tensor_tensor", sBb[:], sv[:, C_BIN:C_BIN + 8],
               sv[:, C_BP:C_BP + 8], ALU.mult, reads=["sv"], writes=["sBb"])

        def prep_weight(wsrc, wdst, nchunks, fwidth, a_sb, a_key, bT_sb, bT_key,
                        col_recip, col_s, tag, wraw, qbuf0, qbuf1,
                        abt_full, abt_full2, chunk_hook=None, preloaded=False):
            if True:
                wbig = wraw[:].rearrange("p (c f) -> p c f", c=nchunks)
                wbig_keys = ([f"wbig_{tag}_g{g}" for g in range(nchunks // 4)]
                             + [f"wbig_{tag}{c}" for c in range(nchunks)])
                wsrc_r = wsrc.rearrange("(c p) f -> p c f", p=128)
                GRP = 2 if nchunks == NCH else 4
                ngrp = nchunks // GRP
                lstream = lambda g: ("l0", "l1", "l2")[g % 3]
                if not preloaded:
                    for g0 in range(min(3, ngrp)):
                        sl = slice(g0 * GRP, (g0 + 1) * GRP)
                        P.emit("q", "dma_start", out=wbig[:, sl, :],
                               in_=wsrc_r[:, sl, :],
                               writes=[f"wbig_{tag}_g{g0}"], sem=lstream(g0))
                    for g0 in range(ngrp):
                        for c in range(g0 * GRP, (g0 + 1) * GRP):
                            P.emit("v", "tensor_reduce", samax[:, c:c + 1],
                                   wbig[:, c, :], AX, ALU.max,
                                   apply_absolute_value=True,
                                   reads=[f"wbig_{tag}_g{g0}"],
                                   writes=["samax"])
                        if g0 + 3 < ngrp:
                            sl = slice((g0 + 3) * GRP, (g0 + 4) * GRP)
                            P.emit("q", "dma_start", out=wbig[:, sl, :],
                                   in_=wsrc_r[:, sl, :],
                                   writes=[f"wbig_{tag}_g{g0 + 3}"],
                                   sem=lstream(g0 + 3))
                P.emit("v", "tensor_reduce", samax[:, 32:33], samax[:, 0:nchunks],
                       AX, ALU.max, apply_absolute_value=True,
                       reads=["samax"], writes=["samax1"])
                P.emit("t", "transpose", pst0[0:1, 0:128], samax[:, 32:33], sid[:],
                       reads=["samax1", "sid"], writes=["pst0"])
                C_S, C_R = srow_ab[0:1, 0:1], srow_ab[0:1, 1:2]
                P.emit("v", "tensor_reduce", C_S, pst0[0:1, 0:128],
                       AX, ALU.max, reads=["pst0"], writes=["srow"])
                P.emit("v", "reciprocal", C_R, C_S,
                       reads=["srow"], writes=["srow"])
                P.emit("v", "tensor_single_scalar", C_R, C_R,
                       127.0, ALU.mult, reads=["srow"], writes=["srow"])
                P.emit("v", "tensor_single_scalar", C_S, C_S,
                       1.0 / 127.0, ALU.mult, reads=["srow"], writes=["srow"])
                P.emit("t", "matmul", pbc[:, 0:1], so1f[0:1, :],
                       C_R,
                       start=True, stop=True, reads=["so1f", "srow"], writes=["pbc"])
                P.emit("v", "tensor_copy", sscal[:, col_recip:col_recip + 1],
                       pbc[:, 0:1], reads=["pbc"], writes=["sscal"])
                P.emit("t", "matmul", pbc[:, 0:1], so1f[0:1, :],
                       C_S,
                       start=True, stop=True, reads=["so1f", "srow"], writes=["pbc"])
                P.emit("v", "tensor_copy", sscal[:, col_s:col_s + 1],
                       pbc[:, 0:1], reads=["pbc"], writes=["sscal"])
                if tag == "pj":
                    for h in range(4):
                        sl = slice(h * D, (h + 1) * D)
                        P.emit("q", "dma_start", out=a_sb[:],
                               in_=aproj[:, sl], reads=[a_key],
                               writes=[a_key], sem="qd")
                        P.emit("v", "tensor_scalar", abf[a_key][:, sl],
                               a_sb[:],
                               sscal[0:16, col_recip:col_recip + 1], 2.0,
                               ALU.mult, op1=ALU.mult,
                               reads=[a_key, "sscal"], writes=[a_key + "b"])
                else:
                    P.emit("v", "tensor_scalar", abf[a_key][:], a_sb[:],
                           sscal[0:16, col_recip:col_recip + 1], 2.0, ALU.mult,
                           op1=ALU.mult, reads=[a_key, "sscal"],
                           writes=[a_key + "b"])

                nseg = fwidth // 512
                for c in range(nchunks):
                    qbuf = (qbuf0 if c % 2 == 0 else qbuf1)[:, 0:fwidth]
                    qk = f"qbuf{c % 2}"
                    abt = (abt_full if c % 2 == 0 else abt_full2)[:, 0:fwidth]
                    abk = "abt" if c % 2 == 0 else "abt2"
                    P.emit("s", "mul", wbig[:, c, :], wbig[:, c, :],
                           sscal[:, col_recip:col_recip + 1],
                           reads=[f"wbig_{tag}_g{c // GRP}", "sscal"],
                           writes=[f"wbig_{tag}{c}"])
                    P.emit("v", "tensor_scalar", qbuf, wbig[:, c, :],
                           MAGIC, MAGIC, ALU.add, op1=ALU.subtract,
                           reads=[f"wbig_{tag}{c}"], writes=[qk])
                    for n in range(nseg):
                        ring = [(pA, "pA"), (pB, "pB"), (pW[0], "pW0"),
                                (pW[1], "pW1"), (pW[2], "pW2"), (pW[3], "pW3")]
                        pX, pk = ring[(c * nseg + n) % 6]
                        sl = slice(n * 512, (n + 1) * 512)
                        P.emit("t", "matmul", pX[:],
                               abf[a_key][:, c * 128:(c + 1) * 128],
                               abf[bT_key][:, sl], start=True, stop=True,
                               reads=[a_key + "b", bT_key + "b"], writes=[pk])
                        P.emit("s", "copy", abt[:, sl], pX[:],
                               reads=[pk], writes=[abk])
                    P.emit("v", "tensor_tensor", qbuf, abt,
                           qbuf, ALU.add,
                           reads=[abk, qk], writes=[qk])
                    P.emit("g", "dma_start",
                           out=wdst[c * 128:(c + 1) * 128, :], in_=qbuf,
                           reads=[qk], writes=[f"wdst_{tag}_{c}"],
                           sem=("wfc" if c % 2 == 0 else "wf2") if tag == "fc"
                               else ("wpj" if c % 2 == 0 else "wp2"))
                    if chunk_hook is not None:
                        chunk_hook(c)

        with nc.sbuf_tensor("safc", [16, D], F32) as safc, \
             nc.sbuf_tensor("safcb", [16, D], BF16) as safcb, \
             nc.sbuf_tensor("sbfcTb", [16, FF], BF16) as sbfcTb, \
             nc.sbuf_tensor("saprojb", [16, FF], BF16) as saprojb, \
             nc.sbuf_tensor("sbprojTb", [16, D], BF16) as sbprojTb, \
             nc.sbuf_tensor("wraw", [128, 32768], F32) as wraw, \
             nc.sbuf_tensor("qbuf0", [128, FF], BF16) as qbuf0, \
             nc.sbuf_tensor("qbuf1", [128, FF], BF16) as qbuf1, \
             nc.sbuf_tensor("abt", [128, FF], BF16) as abt_full, \
             nc.sbuf_tensor("abt2", [128, FF], BF16) as abt_full2:
            fc_keys = ([f"wbig_fc_g{g}" for g in range(4)]
                       + [f"wbig_fc{c}" for c in range(NCH)])
            pj_keys = ([f"wbig_pj_g{g}" for g in range(8)]
                       + [f"wbig_pj{c}" for c in range(NFCH)])
            _abk = [aopen(safc, ["safc"]),
                    aopen(safcb, ["safcb"]), aopen(sbfcTb, ["sbfcTb"]),
                    aopen(saprojb, ["saprojb"]), aopen(sbprojTb, ["sbprojTb"]),
                    aopen(wraw, fc_keys + pj_keys),
                    aopen(qbuf0, ["qbuf0"]), aopen(qbuf1, ["qbuf1"]),
                    aopen(abt_full, ["abt"]), aopen(abt_full2, ["abt2"])]
            P.emit("q", "dma_start", out=safc[:], in_=afc, writes=["safc"],
                   sem="qd")
            small_keys.append("safc")
            P.bump_stream("qd", small_keys)
            abf = {"safc": safcb, "sbfcT": sbfcTb,
                   "saproj": saprojb, "sbprojT": sbprojTb}
            P.emit("g", "dma_start", out=sbfcTb[:], in_=bfcT,
                   writes=["sbfcTb"], sem="b0")
            P.emit("g", "dma_start", out=sbprojTb[:], in_=bprojT,
                   writes=["sbprojTb"], sem="b1")
            with nc.sbuf_tensor("wstg", [128, 1, D], F32) as wstg, \
                 nc.sbuf_tensor("wstgb", [128, 4, D], BF16) as wstgb:
                _tk = [aopen(wstg, ["wstg"]), aopen(wstgb, ["wstgb"])]
                for wsrc, wdst, tg in [(winT, wine, "wi"),
                                       (woutT, woute, "wo")]:
                    wr = wsrc.rearrange("(c p) f -> p c f", p=128)
                    wd = wdst.rearrange("(c p) f -> p c f", p=128)
                    for h in range(2):
                        for j in range(4):
                            c = 4 * h + j
                            P.emit("g", "dma_start", out=wstg[:],
                                   in_=wr[:, c:c + 1, :],
                                   reads=["wstg"], writes=["wstg"], sem="gd")
                            P.emit("s", "activation", wstgb[:, j, :],
                                   wstg[:, 0, :], AF.Identity,
                                   reads=["wstg"], writes=["wstgb"])
                        P.emit("g", "dma_start",
                               out=wd[:, 4 * h:4 * h + 4, :],
                               in_=wstgb[:], reads=["wstgb"],
                               writes=[f"wsc_{tg}"], sem="gd")
                aclose(_tk)
            wpj_src = wprojT.rearrange("(c p) f -> p c f", p=128)
            wbig_pj = wraw[:].rearrange("p (c f) -> p c f", c=NFCH)

            def pj_reduces(g):
                for c in range(g * 4, (g + 1) * 4):
                    P.emit("v", "tensor_reduce", samax[:, c:c + 1],
                           wbig_pj[:, c, :], AX, ALU.max,
                           apply_absolute_value=True,
                           reads=[f"wbig_pj_g{g}"], writes=["samax"])

            def chase(cfc):
                g = cfc
                P.extra.setdefault(f"wbig_pj_g{g}", set()).update(
                    {f"wbig_fc{g}", f"wbig_fc_g{g // 2}"})
                sl = slice(g * 4, (g + 1) * 4)
                P.emit(("q", "q", "g")[g % 3], "dma_start",
                       out=wbig_pj[:, sl, :], in_=wpj_src[:, sl, :],
                       writes=[f"wbig_pj_g{g}"],
                       sem=("l0", "l1", "b0")[g % 3])
                if g >= 2:
                    pj_reduces(g - 2)

            prep_weight(wfcT, wfce, NCH, FF, safc, "safc", None, "sbfcT",
                        0, 1, "fc", wraw, qbuf0, qbuf1, abt_full, abt_full2,
                        chunk_hook=chase)
            pj_reduces(6)
            pj_reduces(7)
            with nc.sbuf_tensor("aptmp", [16, D], F32) as aptmp:
                _apk = [aopen(aptmp, ["saproj"])]
                prep_weight(wprojT, wproje, NFCH, D, aptmp, "saproj",
                            None, "sbprojT", 2, 3, "pj", wraw, qbuf0, qbuf1,
                            abt_full, abt_full2, preloaded=True)
                aclose(_apk)
            aclose(_abk)

        smub = sb("smub", [128, TT], F32)
        sab = sb("sab", [128, TT], F32)
        sz = sb("sz", [128, NCH, 2 * TT], F32)

        def bc3(ap2d):
            return ap2d.rearrange("p (a n) -> p a n", a=1).broadcast_to(
                [128, NCH, TT])

        def colb3(base):
            return sv[:, base:base + 8].rearrange(
                "p (c a) -> p c a", a=1).broadcast_to([128, NCH, TT])

        def decb(c):
            return sdecay[:, c:c + 1].broadcast_to([128, TT])

        def ln_stats(src_ap, src_keys, ssq, sq_key, sob, ob_key, sq_eng="s",
                     mid_hook=None):
            sq_keys = sq_key if isinstance(sq_key, list) else [sq_key]
            ob_keys = ob_key if isinstance(ob_key, list) else [ob_key]
            """Feature-LN stats of src -> broadcast tiles sab (a) and smub (mu*a).

            src_ap: [128, NCH, TT] view (possibly strided). Emits casts on s/g,
            stats matmuls on t, row math spread over s/g/v, broadcasts on t,
            evacuations on s.
            """
            if sq_eng == "s":
                P.emit("s", "activation", ssq[:], src_ap, AF.Square,
                       reads=src_keys, writes=sq_keys)
            else:
                P.emit("v", "tensor_tensor", ssq[:], src_ap, src_ap,
                       ALU.mult, reads=src_keys, writes=sq_keys)
            P.emit("v", "tensor_copy", sob[:], src_ap,
                   reads=src_keys, writes=ob_keys)
            for c in range(NCH):
                P.emit("t", "matmul", pst0[0:1, :], socd[:], sob[:, c, :],
                       start=(c == 0), stop=(c == NCH - 1), accum=(c > 0),
                       tile_position=(0, 0),
                       reads=["socd"] + ob_keys, writes=["pst0"])
                P.emit("t", "matmul", pst0[32:33, :], socd[:], ssq[:, c, :],
                       start=(c == 0), stop=(c == NCH - 1), accum=(c > 0),
                       tile_position=(0, 32),
                       reads=["socd"] + sq_keys, writes=["pst0"])
            MU, MSQ, VAR = pst0[0:1, :], pst0[32:33, :], pst0[96:97, :]
            A_, MUA = srow_ab[0:1, :], srow_ab[32:33, :]
            P.emit("s", "activation", A_, MU, AF.Square,
                   reads=["pst0"], writes=["srow"])
            P.emit("v", "tensor_tensor", VAR, MSQ, A_,
                   ALU.subtract, reads=["pst0", "srow"], writes=["pst0"])
            P.emit("s", "activation", A_, VAR, AF.Ln,
                   bias=np.float32(EPS), reads=["pst0"], writes=["srow"])
            P.emit("s", "activation", A_, A_, AF.Exp,
                   scale=np.float32(-0.5), reads=["srow"], writes=["srow"])
            P.emit("v", "tensor_tensor", MUA, MU, A_,
                   ALU.mult, reads=["pst0", "srow"], writes=["srow"])
            if mid_hook is not None:
                mid_hook()
            P.emit("t", "matmul", pbc[:], so1f[0:1, :], A_,
                   start=True, stop=True, reads=["so1f", "srow"], writes=["pbc"])
            P.emit("s", "copy", sab[:], pbc[:], reads=["pbc"], writes=["sab"])
            P.emit("t", "matmul", pbc[:], so1f[32:33, :], MUA,
                   start=True, stop=True, reads=["so1f", "srow"], writes=["pbc"])
            P.emit("s", "copy", smub[:], pbc[:], reads=["pbc"], writes=["smub"])

        sxt1 = sb("sxt1", [128, NCH, TT], F32)
        sbfA1 = sb("sbfA1", [128, NCH, TT], BF16)
        sut1 = sb("sut1", [128, NCH, TT], BF16)
        swin = sb("swin", [128, NCH, D], BF16)
        sstb = sb("sstb", [128, NCH, TT], F32)
        sxt0 = sb("sxt0", [128, NCH, TT], F32)
        sbfA0 = sb("sbfA0", [128, NCH, TT], BF16)
        sut0 = sb("sut0", [128, NCH, TT], BF16)
        swout = sb("swout", [128, NCH, D], BF16)
        sot = sb("sot", [128, NCH, TT], F32)
        sscr = sb("sscr", [128, NCH, TT], BF16)
        sG = sb("sG", [128, 16, TT], BF16)
        s1 = sb("s1", [128, NCH, 1024], BF16)
        assert nc.sbuf_base <= 229344, nc.sbuf_base

        addr = lambda h: nc.lookup_mloc(h).addr
        assert addr(sut0) == addr(sbfA0) + 8192
        assert addr(sut1) == addr(sbfA1) + 8192
        gat = nc.alloc_sbuf_tensor_at
        gwin = gat("gwin", [128, NCH, 1024], BF16, offset=addr(swin))
        gwout = gat("gwout", [128, NCH, 1024], BF16, offset=addr(swout))
        gx0 = gat("gx0", [128, NCH, 1024], BF16, offset=addr(sxt0))
        gb0 = gat("gb0", [128, NCH, 1024], BF16, offset=addr(sbfA0))
        gx1 = gat("gx1", [128, NCH, 1024], BF16, offset=addr(sxt1))
        gb1 = gat("gb1", [128, NCH, 1024], BF16, offset=addr(sbfA1))
        gq1a = gat("gq1a", [128, 4, 1024], BF16, offset=addr(sut0))
        gq1b = gat("gq1b", [128, 4, 1024], BF16, offset=addr(sut1))
        gsq = gat("gsq", [128, NCH, TT], BF16, offset=addr(sstb))
        gob = gat("gob", [128, NCH, TT], BF16, offset=addr(sstb) + 8192)
        gaf = gat("gaf", [128, NCH, TT], F32, offset=addr(sot))

        sxt2, sbfA2, sut2 = [sxt0, sxt1], [sbfA0, sbfA1], [sut0, sut1]
        su0k = [f"su0_{e}" for e in range(NCH)]
        su1k = [f"su1_{e}" for e in range(NCH)]
        sstk = [f"sst{c}" for c in range(NCH)]
        afk = [f"af{j}" for j in range(NCH)]
        fck = [f"wdst_fc_{c}" for c in range(NCH)]
        pjk = [f"wdst_pj_{c}" for c in range(NFCH)]
        _pk = [aopen(swin, ["swin"]), aopen(swout, ["swout"]),
               aopen(sxt0, ["sxt0"]), aopen(sxt1, ["sxt1"]),
               aopen(sbfA0, ["bfA0"]), aopen(sbfA1, ["bfA1"]),
               aopen(sut0, su0k), aopen(sut1, su1k),
               aopen(sstb, sstk), aopen(sot, ["sot"]),
               aopen(sscr, ["sscr0", "sscr1"]), aopen(sG, ["sG"]),
               aopen(s1, ["s1"]), aopen(smub, ["smub"]),
               aopen(sab, ["sab"]),
               aopen(sz, [f"sz{i}_{c}" for i in range(2) for c in range(NCH)])]

        xr = xT.rearrange("(c p) n -> p c n", p=128)
        wfc_d = wfce.rearrange("(c p) f -> p c f", p=128)
        wpj_d = wproje.rearrange("(c p) f -> p c f", p=128)
        win_d = wine.rearrange("(c p) f -> p c f", p=128)
        wout_d = woute.rearrange("(c p) f -> p c f", p=128)
        yout_d = youtT.rearrange("(c p) n -> p c n", p=128)

        def load_x(n, src_ap):
            p = n % 2
            P.emit("q", "dma_start", out=sxt2[p][:], in_=src_ap,
                   reads=[f"sxt{p}", f"bfA{p}"], writes=[f"sxt{p}"], sem="xd")

        def stage_A_evac(p, e):
            pX, pk = (pA, "pA") if e % 2 == 0 else (pB, "pB")
            P.emit("s", "activation", sut2[p][:, e, :], pX[:],
                   AF.Identity, bias=sBb[:, e:e + 1], scale=col(C_BP, e),
                   reads=[pk, "sBb", "sv"], writes=[f"su{p}_{e}"])

        def stage_A(n, add_beta=True, es=None, emit_cast=True,
                    emit_evac=True):
            p = n % 2
            xk, bk = f"sxt{p}", f"bfA{p}"
            if emit_cast:
                P.emit("s", "activation", sbfA2[p][:], sxt2[p][:],
                       AF.Identity, reads=[xk], writes=[bk])
            if add_beta:
                for c in range(NCH):
                    P.emit("s", "activation", sxt2[p][:, c, :],
                           sxt2[p][:, c, :], AF.Identity,
                           bias=col(C_BET, c),
                           reads=[xk, "sv"], writes=[xk])
            for e in (range(NCH) if es is None else es):
                pX, pk = (pA, "pA") if e % 2 == 0 else (pB, "pB")
                for k in range(NCH):
                    P.emit("t", "matmul", pX[:],
                           swin[:, k, e * 128:(e + 1) * 128],
                           sbfA2[p][:, k, :],
                           start=(k == 0), stop=(k == NCH - 1),
                           accum=(k > 0), reads=["swin", bk], writes=[pk])
                if emit_evac:
                    stage_A_evac(p, e)

        def stage_B(n):
            p = n % 2
            for c in range(NCH):
                P.emit("v", "tensor_tensor_scan", sstb[:, c, :],
                       sdecay[:, c:c + 1].broadcast_to([128, TT]),
                       sut2[p][:, c, :], smcol[:, c:c + 1],
                       ALU.mult, ALU.add,
                       reads=["sdecay", f"su{p}_{c}", f"sst{c}", f"smcol{c}"],
                       writes=[f"sst{c}"])
            P.emit("v", "tensor_copy", smcol[:, 0:NCH],
                   sstb[:, :, TT - 1:TT].rearrange("p c a -> p (c a)"),
                   reads=sstk,
                   writes=[f"smcol{c}" for c in range(NCH)])

        def stage_y(n):
            p = n % 2
            P.emit("v", "tensor_tensor", sbfA2[p][:], sstb[:], colb3(C_CP),
                   ALU.mult, reads=sstk + ["sv"], writes=[f"bfA{p}"])

        def stage_C_evac(e):
            i = e % 4
            P.emit("s", "activation", sot[:, e, :], pW[i][:],
                   AF.Identity, bias=col(C_BOUT, e),
                   reads=[f"pW{i}", "sv"], writes=["sot"])

        def stage_C(n, es=None, emit_evac=True):
            p = n % 2
            bk = f"bfA{p}"
            for e in (range(NCH) if es is None else es):
                i = e % 4
                for k in range(NCH):
                    P.emit("t", "matmul", pW[i][:],
                           swout[:, k, e * 128:(e + 1) * 128],
                           sbfA2[p][:, k, :],
                           start=(k == 0), stop=(k == NCH - 1),
                           accum=(k > 0), reads=["swout", bk], writes=[f"pW{i}"])
                if emit_evac:
                    stage_C_evac(e)

        def stage_D_head(p, mid_hook=None):
            ln_stats(sot[:], ["sot"], gsq, sstk[0:4], gob[:], sstk[4:8],
                     mid_hook=mid_hook)
            for c in range(NCH):
                P.emit("v", "scalar_tensor_tensor", sot[:, c, :],
                       sot[:, c, :], col(C_GAM, c), sab[:],
                       ALU.mult, ALU.mult,
                       reads=["sot", "sv", "sab"], writes=["sot"])
            for c in range(NCH):
                P.emit("v", "scalar_tensor_tensor", sstb[:, c, :],
                       smub[:], col(C_GAM, c), sxt2[p][:, c, :],
                       ALU.mult, ALU.subtract,
                       reads=["smub", "sv", f"sxt{p}"] + sstk,
                       writes=[f"sst{c}"])
            P.emit("v", "tensor_tensor", sxt2[p][:], sot[:], sstb[:],
                   ALU.subtract, reads=["sot"] + sstk, writes=[f"sxt{p}"])

        def stage_D_tail(p, it, mid_hook=None):
            zsl = slice(it * TT, (it + 1) * TT)
            zkeys = [f"sz{it}_{c}" for c in range(NCH)]
            ln_stats(sxt2[p][:], [f"sxt{p}"], gsq, sstk[0:4], gob[:],
                     sstk[4:8], sq_eng="v", mid_hook=mid_hook)
            for h in range(2):
                cs = slice(4 * h, 4 * h + 4)
                hb = lambda ap2d: ap2d.rearrange(
                    "p (a n) -> p a n", a=1).broadcast_to([128, 4, TT])
                P.emit("v", "tensor_tensor", sz[:, cs, zsl], sxt2[p][:, cs, :],
                       hb(sab[:]), ALU.mult,
                       reads=[f"sxt{p}", "sab"], writes=zkeys[4 * h:4 * h + 4])
                P.emit("v", "tensor_tensor", sz[:, cs, zsl], sz[:, cs, zsl],
                       hb(smub[:]), ALU.subtract,
                       reads=zkeys[4 * h:4 * h + 4] + ["smub"],
                       writes=zkeys[4 * h:4 * h + 4])

        FCRING = [(pW[0], "pW0"), (pW[1], "pW1"), (pA, "pA"), (pB, "pB")]
        PJRING = [(pW[2], "pW2"), (pW[3], "pW3"), (pA, "pA"), (pB, "pB")]

        def fc_quarter(q, pieces, jjs=None):
            for jj in (range(8) if jjs is None else jjs):
                m = 8 * q + jj
                pX, pk = FCRING[m % 4]
                for k in range(NCH):
                    sl, ky, klo = next((s, y, lo) for (s, y, lo, hi) in pieces
                                       if lo <= k < hi)
                    P.emit("t", "matmul", pX[:],
                           sl[:, k - klo, jj * 128:(jj + 1) * 128],
                           sscr[:, k, :],
                           start=(k == 0), stop=(k == NCH - 1),
                           accum=(k > 0),
                           reads=[ky, f"sscr{k // 4}"], writes=[pk])
                P.emit("s", "activation", sG[:, m % 16, :], pX[:], AF.Gelu,
                       bias=sbfc[:, m:m + 1], scale=sscal[:, 1:2],
                       reads=[pk, "sbfc", "sscal"], writes=["sG"])

        def pj_half(sub, slotL, keyL, slotH, keyH, first, js=None,
                    emit_evac=True):
            zsl = slice(sub * TT, (sub + 1) * TT)
            for j in (range(NCH) if js is None else js):
                pX, pk = PJRING[j % 4]
                korder = (list(range(8, 16)) + list(range(8))) if first \
                    else list(range(16))
                for idx, k in enumerate(korder):
                    sl, ky = (slotL, keyL) if k < 8 else (slotH, keyH)
                    P.emit("t", "matmul", pX[:],
                           sl[:, k % 8, j * 128:(j + 1) * 128], sG[:, k, :],
                           start=(idx == 0), stop=(idx == 15),
                           accum=(idx > 0), reads=[ky, "sG"], writes=[pk])
                if emit_evac:
                    pj_evac(sub, j, first)

        def pj_evac(sub, j, first):
            zsl = slice(sub * TT, (sub + 1) * TT)
            pX, pk = PJRING[j % 4]
            if first:
                P.emit("s", "activation", gaf[:, j, :], pX[:], AF.Identity,
                       bias=sbpb[:, j:j + 1],
                       reads=[pk, "sbpb"], writes=[afk[j]])
            else:
                P.emit("v", "tensor_tensor", gaf[:, j, :], pX[:],
                       gaf[:, j, :], ALU.add,
                       reads=[pk, afk[j]], writes=[afk[j]])
                P.emit("v", "scalar_tensor_tensor", sz[:, j, zsl],
                       gaf[:, j, :], sscal[:, 3:4], sz[:, j, zsl],
                       ALU.mult, ALU.add,
                       reads=[afk[j], "sscal", f"sz{sub}_{j}"],
                       writes=[f"sz{sub}_{j}"])

        def szb_cast(sub):
            zsl = slice(sub * TT, (sub + 1) * TT)
            for h in range(2):
                cs = slice(4 * h, 4 * h + 4)
                P.emit("s", "activation", sscr[:, cs, :], sz[:, cs, zsl],
                       AF.Identity,
                       reads=[f"sz{sub}_{c}" for c in range(4 * h, 4 * h + 4)]
                       + [f"sscr{h}"],
                       writes=[f"sscr{h}"])

        def prebias(sub):
            zsl = slice(sub * TT, (sub + 1) * TT)
            for c in range(NCH):
                P.emit("s", "activation", sz[:, c, zsl], sz[:, c, zsl],
                       AF.Identity, bias=col(C_BPJ, c),
                       reads=[f"sz{sub}_{c}", "sv", "sscr0", "sscr1"],
                       writes=[f"sz{sub}_{c}"])

        def ln3_fire(r, sub, mid_hook=None):
            zsl = slice(sub * TT, (sub + 1) * TT)
            zkeys = [f"sz{sub}_{c}" for c in range(NCH)]
            toff = r * 2 * TT + sub * TT
            ln_stats(sz[:, :, zsl], zkeys, gsq, sstk[0:4], gob[:], sstk[4:8],
                     mid_hook=mid_hook)
            for h in range(2):
                cs = slice(4 * h, 4 * h + 4)
                hb = lambda ap2d: ap2d.rearrange(
                    "p (a n) -> p a n", a=1).broadcast_to([128, 4, TT])
                hk = afk[4 * h:4 * h + 4]
                P.emit("v", "tensor_tensor", gaf[:, cs, :], sz[:, cs, zsl],
                       hb(sab[:]), ALU.mult,
                       reads=zkeys[4 * h:4 * h + 4] + ["sab"], writes=hk)
                P.emit("v", "tensor_tensor", gaf[:, cs, :], gaf[:, cs, :],
                       hb(smub[:]), ALU.subtract,
                       reads=hk + ["smub"], writes=hk)
                P.emit("g", "dma_start", out=yout_d[:, cs, toff:toff + TT],
                       in_=gaf[:, cs, :], reads=hk,
                       writes=[f"yout{r}{sub}{h}"], sem="out")

        def wload(eng, sem, dst, src, wkeys, hkeys):
            P.emit(eng, "dma_start", out=dst[:], in_=src,
                   reads=wkeys + hkeys, writes=hkeys, sem=sem)

        def emit_mega(r):
            for j in range(NCH):
                P.extra.setdefault(afk[j], set()).add("sot")
            P.extra.setdefault("sot", set()).update(afk)
            for e in range(NCH):
                P.extra.setdefault(f"su0_{e}", set()).add("bfA0")
                P.extra.setdefault(f"su1_{e}", set()).add("bfA1")

            t0 = r * 2 * TT
            ldq = "s" if r == 0 else "q"
            ldg = "s" if r == 0 else "g"

            def load_x0(n, src_ap):
                p = n % 2
                P.emit(ldq, "dma_start", out=sxt2[p][:], in_=src_ap,
                       reads=[f"sxt{p}", f"bfA{p}"], writes=[f"sxt{p}"],
                       sem="xd")

            if r == 0:
                load_x0(1, xhT.rearrange("(c p) n -> p c n", p=128))
                load_x0(0, xr[:, :, t0:t0 + TT])
                wload(ldq, "wd", swin, win_d, ["wsc_wi"], ["swin"])
            if r > 0:
                load_x0(1, xr[:, :, t0 + TT:t0 + 2 * TT])
                wload("g", "gd", swout, wout_d, ["wsc_wo"], ["swout"])
            if r == 0:
                stage_A(-1, add_beta=False)
                for c in range(NCH):
                    P.emit("v", "tensor_tensor_scan", sstb[:, c, :],
                           sdecay[:, c:c + 1].broadcast_to([128, TT]),
                           sut1[:, c, :], 0.0, ALU.mult, ALU.add,
                           reads=["sdecay", f"su1_{c}", f"sst{c}"],
                           writes=[f"sst{c}"])
                for c in range(NCH):
                    P.emit("v", "tensor_tensor", smcol[:, c:c + 1],
                           sstb[:, c, HALO - 1:HALO], shm[:], ALU.mult,
                           reads=[f"sst{c}", "shm"], writes=[f"smcol{c}"])
                load_x0(1, xr[:, :, t0 + TT:t0 + 2 * TT])
            if r == 0:
                stage_A(0)
            else:
                for _e in (0, 1):
                    stage_A_evac(0, _e)
                stage_A(0, emit_cast=False, es=range(2, NCH))
            stage_A(1)
            if r == 0:
                wload(ldg, "wd", swout, wout_d, ["wsc_wo"], ["swout"])
            wload("q", "wd", s1, wfc_d[:, :, 0:1024], fck, ["s1"])
            wload("q", "wd", gwin, wfc_d[:, :, 2048:3072], fck, ["swin"])
            stage_B(0)
            wload("g", "gd", gq1a, wfc_d[:, 0:4, 1024:2048], fck, su0k)
            stage_y(0)
            stage_C(0)
            stage_B(1)
            wload("g", "gd", gq1b, wfc_d[:, 4:8, 1024:2048], fck, su1k)
            stage_y(1)
            stage_D_head(0, mid_hook=lambda: stage_C(
                1, es=[0, 1], emit_evac=False))
            stage_C_evac(0)
            stage_C_evac(1)
            stage_C(1, es=range(2, 6))
            stage_D_tail(0, 0, mid_hook=lambda: stage_C(
                1, es=[6, 7], emit_evac=False))
            stage_C_evac(6)
            stage_C_evac(7)
            wload("g", "gd", gwout, wpj_d[:, 8:16, :], pjk[8:16],
                  ["swout"])
            szb_cast(0)
            wload("q", "wd", gx0, wpj_d[:, 0:8, :], pjk[0:8], ["sxt0"])
            fc_quarter(0, [(s1, "s1", 0, 8)])
            wload("q", "wd", s1, wfc_d[:, :, 3072:4096], fck, ["s1"])
            fc_quarter(1, [(gq1a, su0k[0], 0, 4), (gq1b, su1k[0], 4, 8)])
            wload("g", "gd", gb0, wpj_d[:, 16:24, :], pjk[16:24],
                  ["bfA0"] + su0k)
            wload("g", "gd", gb1, wfc_d[:, :, 1024:2048], fck,
                  ["bfA1"] + su1k)
            stage_D_head(1, mid_hook=lambda: pj_half(
                0, gx0, "sxt0", gwout, "swout", first=True, js=[0, 1],
                emit_evac=False))
            pj_evac(0, 0, True)
            pj_evac(0, 1, True)
            pj_half(0, gx0, "sxt0", gwout, "swout", first=True,
                    js=range(2, NCH))
            fc_quarter(2, [(gwin, "swin", 0, 8)], jjs=[0, 1])
            stage_D_tail(1, 1, mid_hook=lambda: fc_quarter(
                2, [(gwin, "swin", 0, 8)], jjs=[2, 3]))
            wload("q", "wd", gx1, wpj_d[:, 24:32, :], pjk[24:32],
                  ["sxt1"])
            fc_quarter(2, [(gwin, "swin", 0, 8)], jjs=range(4, 8))
            wload("q", "wd", gx0, wpj_d[:, 0:8, :], pjk[0:8], ["sxt0"])
            wload("g", "gd", gwout, wpj_d[:, 8:16, :], pjk[8:16],
                  ["swout"])
            fc_quarter(3, [(s1, "s1", 0, 8)])
            szb_cast(1)
            pj_half(0, gb0, "bfA0", gx1, "sxt1", first=False)
            wload("q", "wd", s1, wfc_d[:, :, 0:1024], fck, ["s1"])
            wload("q", "wd", gwin, wfc_d[:, :, 2048:3072], fck,
                  ["swin"])
            fc_quarter(0, [(s1, "s1", 0, 8)])
            fc_quarter(1, [(gb1, "bfA1", 0, 8)])
            ln3_fire(r, 0, mid_hook=lambda: pj_half(
                1, gx0, "sxt0", gwout, "swout", first=True, js=[0, 1],
                emit_evac=False))
            pj_evac(1, 0, True)
            pj_evac(1, 1, True)
            wload("g", "gd", gb0, wpj_d[:, 16:24, :], pjk[16:24],
                  ["bfA0"] + su0k)
            pj_half(1, gx0, "sxt0", gwout, "swout", first=True,
                    js=range(2, NCH))
            wload("g", "gd", gb1, wpj_d[:, 24:32, :], pjk[24:32],
                  ["bfA1"] + su1k)
            fc_quarter(2, [(gwin, "swin", 0, 8)])
            wload("q", "wd", s1, wfc_d[:, :, 3072:4096], fck, ["s1"])
            fc_quarter(3, [(s1, "s1", 0, 8)])
            pj_half(1, gb0, "bfA0", gb1, "bfA1", first=False)
            if r + 1 < ROUNDS:
                t1n = (r + 1) * 2 * TT
                load_x(0, xr[:, :, t1n:t1n + TT])
                wload("q", "wd", swin, win_d, ["wsc_wi"], ["swin"])
                stage_A(0, add_beta=False, es=[0], emit_evac=False)
                ln3_fire(r, 1, mid_hook=lambda: stage_A(
                    0, add_beta=False, es=[1], emit_cast=False,
                    emit_evac=False))
            else:
                ln3_fire(r, 1)

        P.emit("v", "tensor_scalar", sbpb[:], sv[:, C_BPJ:C_BPJ + 8],
               sscal[:, 2:3], 1.0, ALU.mult, op1=ALU.mult,
               reads=["sv", "sscal"], writes=["sbpb"])

        for r in range(ROUNDS):
            emit_mega(r)

        total_out = P.cnt["out"]

        def final_wait(eng, total_out=total_out):
            eng.wait_ge(P.sem["out"], total_out)
        P.raw("g", final_wait)

        with nc.Block() as block:
            @block.sync
            def _(eng):
                for f in P.ops["q"]:
                    f(eng)

            @block.tensor
            def _(eng):
                for f in P.ops["t"]:
                    f(eng)

            @block.vector
            def _(eng):
                for f in P.ops["v"]:
                    f(eng)

            @block.scalar
            def _(eng):
                for f in P.ops["s"]:
                    f(eng)

            @block.gpsimd
            def _(eng):
                for f in P.ops["g"]:
                    f(eng)

    return nc


def make_inputs_for_core(core, arrs):
    (x, w_in, b_in, log_a, b_p, c_p, w_out, b_out, gamma, beta,
     w_fc, b_fc, a_fc, b_fcm, w_proj, b_proj, a_proj, b_projm) = arrs
    b = core // 4
    t0 = (core % 4) * TPC
    xT = np.ascontiguousarray(x[b, t0:t0 + TPC, :].T)
    if t0 == 0:
        xhT = np.zeros((D, HALO), np.float32)
        hm = np.zeros((128, 1), np.float32)
    else:
        xhT = np.ascontiguousarray(x[b, t0 - HALO:t0, :].T)
        hm = np.ones((128, 1), np.float32)

    def packcols(*vecs):
        cols = []
        for v in vecs:
            cols.append(v.reshape(NCH, 128).T)
        return np.ascontiguousarray(np.concatenate(cols, axis=1))

    cvec = packcols(b_in, b_out, gamma, beta, b_proj, log_a.reshape(-1),
                    b_p.reshape(-1), c_p.reshape(-1))
    bfc2 = np.ascontiguousarray(b_fc.reshape(32, 128).T)
    return {
        "xT": xT, "xhT": xhT, "hmask": hm,
        "winT": np.ascontiguousarray(w_in.T),
        "woutT": np.ascontiguousarray(w_out.T),
        "wfcT": np.ascontiguousarray(w_fc.T),
        "wprojT": np.ascontiguousarray(w_proj.T),
        "afc": np.ascontiguousarray(a_fc),
        "bfcT": np.ascontiguousarray(b_fcm.T),
        "aproj": np.ascontiguousarray(a_proj),
        "bprojT": np.ascontiguousarray(b_projm.T),
        "cvec": cvec, "bfc2": bfc2,
        "ident": np.eye(128, dtype=np.float32),
        "ones1f": np.ones((1, 128), np.float32),
        "onescb": np.ones((128, 1), dtype=ml_dtypes.bfloat16),
    }


def kernel(L0_attn_x, L0_attn_W_in, L0_attn_b_in, L0_attn_log_A, L0_attn_B_p,
           L0_attn_C_p, L0_attn_W_out, L0_attn_b_out, L0_attn_gamma,
           L0_attn_beta, L0_ffn_W_fc, L0_ffn_b_fc, L0_ffn_A_fc, L0_ffn_B_fc,
           L0_ffn_W_proj, L0_ffn_b_proj, L0_ffn_A_proj, L0_ffn_B_proj):
    arrs = [np.asarray(a, dtype=np.float32) for a in (
        L0_attn_x, L0_attn_W_in, L0_attn_b_in, L0_attn_log_A, L0_attn_B_p,
        L0_attn_C_p, L0_attn_W_out, L0_attn_b_out, L0_attn_gamma, L0_attn_beta,
        L0_ffn_W_fc, L0_ffn_b_fc, L0_ffn_A_fc, L0_ffn_B_fc, L0_ffn_W_proj,
        L0_ffn_b_proj, L0_ffn_A_proj, L0_ffn_B_proj)]
    nc = build_nc()
    in_maps = [make_inputs_for_core(c, arrs) for c in range(NCORES)]
    res = run_bass_kernel_spmd(nc, in_maps, core_ids=list(range(NCORES)))
    out = np.empty((B, S, D), np.float32)
    for c in range(NCORES):
        bidx = c // 4
        t0 = (c % 4) * TPC
        out[bidx, t0:t0 + TPC, :] = np.asarray(res.results[c]["youtT"]).T
    return out
